# version 02
# baseline (speedup 1.0000x reference)
from contextlib import ExitStack

import numpy as np

import concourse.bass as bass
import concourse.mybir as mybir
from concourse.bass_utils import run_bass_kernel_spmd

F32 = mybir.dt.float32
BF16 = mybir.dt.bfloat16
ALU = mybir.AluOpType
AF = mybir.ActivationFunctionType
AX = mybir.AxisListType.X
MAGIC = float(np.float32(1.5 * 2 ** 23))
NCORES = 8
SEQ = 2048
D = 1024
INNER = 2048
FF = 3072
NTOK = 2 * SEQ
QMAX = 32767.0
EPS = 1e-5


class Sem:
    __slots__ = ("h", "n")

    def __init__(self, h):
        self.h = h
        self.n = 0

    def inc(self, inst, k=1):
        inst.then_inc(self.h, k)
        self.n += k
        return (self, self.n)

    def tok(self):
        return (self, self.n)


class Ctx:
    def __init__(self, name):
        self.nc = bass.Bass("TRN2", target_bir_lowering=False, name=name)
        self.es = ExitStack()
        self._sems = {}

    def sem(self, name) -> Sem:
        if name not in self._sems:
            self._sems[name] = Sem(self.es.enter_context(self.nc.semaphore(name)))
        return self._sems[name]

    def sbuf(self, name, shape, dtype, **kw):
        return self.es.enter_context(self.nc.sbuf_tensor(name, shape, dtype, **kw))

    def psum(self, name, shape, dtype=F32):
        return self.es.enter_context(self.nc.psum_tensor(name, shape, dtype))

    def dram(self, name, shape, dtype=F32, kind="Internal", **kw):
        return self.nc.dram_tensor(name, shape, dtype, kind=kind, **kw).ap()

    def ex(self, eng, selfname, f, deps=()):
        s = self.sem(selfname)
        for d in deps:
            if d is not None:
                eng.wait_ge(d[0].h, d[1])
        if s.n:
            eng.wait_ge(s.h, s.n)
        return s.inc(f())

    def dma(self, eng, semname, out, in_, deps=(), **kw):
        s = self.sem(semname)
        for d in deps:
            if d is not None:
                eng.wait_ge(d[0].h, d[1])
        return s.inc(eng.dma_start(out=out, in_=in_, **kw), 16)


V = "v_self"
A = "a_self"
P = "p_self"
T = "t_self"


def emit_consts(ctx, block, values):
    cb = ctx.sbuf("cconsts", [128, max(len(values), 1)], F32)
    aps = {}
    tok = [None]

    @block.vector
    def _(v):
        for i, val in enumerate(values):
            tok[0] = ctx.ex(v, V, lambda i=i, val=val: v.memset(cb[:, i:i + 1],
                                                               float(val)))
            aps[val] = cb[:, i:i + 1]
    return aps, tok[0]



def emit_scalar_prep(ctx, block, m_in_sb, n_in, exprs, scb, ps, ones, dep_ld):
    nc = ctx.nc
    scal = ctx.sbuf("scal_" + m_in_sb.name, [1, 8], F32)

    toks = {}

    @block.vector
    def _(v):
        ctx.ex(v, V, lambda: v.memset(scal[:], 0.0))
        t = ctx.ex(v, V, lambda: v.reduce_max(scal[:, 1:2], m_in_sb[0:1, 0:n_in],
                                              axis=AX), deps=[dep_ld])
        t = ctx.ex(v, V, lambda: v.tensor_scalar(
            out=scal[:, 1:2], in0=scal[:, 1:2], scalar1=float(np.float32(1.0) / np.float32(QMAX)),
            scalar2=1e-12, op0=ALU.mult, op1=ALU.max))
        t = ctx.ex(v, V, lambda: v.reciprocal(scal[:, 0:1], scal[:, 1:2]))
        for (dst, src, s1, op) in exprs:
            t = ctx.ex(v, V, lambda dst=dst, src=src, s1=s1, op=op:
                       v.tensor_scalar(out=scal[:, dst:dst + 1],
                                       in0=scal[:, src:src + 1],
                                       scalar1=s1, scalar2=None, op0=op))
        toks["v"] = t

    @block.tensor
    def _(t):
        tk = ctx.ex(t, T, lambda: t.matmul(ps[:, 0:8], ones[0:1, :],
                                           scal[0:1, 0:8], start=True, stop=True),
                    deps=[toks["v"]])
        toks["t"] = tk

    @block.scalar
    def _(a):
        toks["a"] = ctx.ex(a, A, lambda: a.copy(scb[:, 0:8], ps[:, 0:8]),
                           deps=[toks["t"]])

    return toks["a"]


def emit_colmax_to_out(ctx, block, mxcol, ident, out_dram, n_out, dep_tok,
                       tag, ident_tok=None, ps_view=None, extra_dep=None):
    psx = (ps_view[0:max(n_out, 1), 0:128] if ps_view is not None
           else ctx.psum("psx_" + tag, [max(n_out, 1), 128], F32))
    scratch = ctx.__dict__.get("colmax_scratch")
    if scratch is not None:
        row = scratch[0:max(n_out, 1), 0:128]
        res = scratch[0:max(n_out, 1), 128:129]
    else:
        row = ctx.sbuf("mrow_" + tag, [max(n_out, 1), 128], F32)
        res = ctx.sbuf("mres_" + tag, [max(n_out, 1), 1], F32)
    toks = {}

    @block.tensor
    def _(t):
        toks["t"] = ctx.ex(t, T, lambda: t.transpose(psx[:, :], mxcol[:, 0:n_out],
                                                     ident[:]),
                           deps=[dep_tok, ident_tok, extra_dep])

    @block.vector
    def _(v):
        tk = ctx.ex(v, V, lambda: v.tensor_copy(row[:], psx[:, :]),
                    deps=[toks["t"]])
        toks["v"] = ctx.ex(v, V, lambda: v.reduce_max(res[:], row[:], axis=AX))

    @block.sync
    def _(sync):
        return_tok = ctx.dma(sync, "st_" + tag, out_dram, res[0:n_out, 0:1],
                             deps=[toks["v"]])
        toks["st"] = return_tok
    return toks["st"]


def build_L3():
    ctx = Ctx("L3")
    nc = ctx.nc
    gate = ctx.dram("gate", [FF, NTOK], kind="ExternalInput")
    m3all = ctx.dram("m3all", [1, 8], kind="ExternalInput")
    c_ident = ctx.dram("c_ident", [128, 128], kind="ExternalInput")
    gate2 = ctx.dram("gate2", [FF, NTOK], kind="ExternalOutput")
    m4loc = ctx.dram("m4loc", [1, 1], kind="ExternalOutput")

    n_rows = FF // 128

    with ctx.es:
        sm3 = ctx.sbuf("sm3", [1, 8], F32)
        ones = ctx.sbuf("ones", [1, 128], F32)
        ident = ctx.sbuf("ident", [128, 128], F32)
        scb = ctx.sbuf("scb", [128, 8], F32)
        gin = [ctx.sbuf(f"gin{i}", [128, NTOK], F32) for i in range(2)]
        t_r = [ctx.sbuf(f"t_r{i}", [128, NTOK], F32) for i in range(2)]
        t_sig = [ctx.sbuf(f"t_sig{i}", [128, NTOK], F32) for i in range(2)]
        gout = [ctx.sbuf(f"gout{i}", [128, NTOK], F32) for i in range(2)]
        mx = ctx.sbuf("mx", [128, 1], F32)
        mxr = ctx.sbuf("mxr", [128, 1], F32)
        ps = ctx.psum("ps", [128, 8], F32)

        block = ctx.es.enter_context(nc.Block())

        @block.sync
        def _(sync):
            ctx.dma(sync, "ld0", sm3[:], m3all)
            ctx.dma(sync, "ld0", ident[:], c_ident)

        @block.vector
        def _(v):
            ctx.ex(v, V, lambda: v.memset(ones[:], 1.0))
            ctx.ex(v, V, lambda: v.memset(mx[:], 0.0))

        scb_tok = emit_scalar_prep(
            ctx, block, sm3, 8,
            [(2, 1, 1.0 / 65536.0, ALU.mult)],
            scb, ps, ones, ctx.sem("ld0").tok())

        ld_toks = [None, None]
        r_toks = [None] * n_rows
        sig_toks = [None] * n_rows
        g2_toks = [None] * n_rows
        st_toks = [None, None]
        mx_tok = [None]

        for i in range(n_rows):
            p = i % 2

            @block.sync
            def _(sync, i=i, p=p):
                deps = [r_toks[i - 2] if i >= 2 else None,
                        st_toks[p] if i >= 2 else None]
                ld_toks[p] = ctx.dma(sync, f"ldg{p}", gin[p][:],
                                     gate[i * 128:(i + 1) * 128, :], deps=deps)

            @block.vector
            def _(v, i=i, p=p):
                ctx.ex(v, V, lambda: v.tensor_scalar(
                    out=t_r[p][:], in0=gin[p][:], scalar1=scb[:, 0:1],
                    scalar2=MAGIC, op0=ALU.mult, op1=ALU.add),
                    deps=[ld_toks[p], scb_tok,
                          sig_toks[i - 2] if i >= 2 else None])
                r_toks[i] = ctx.ex(v, V, lambda: v.tensor_scalar(
                    out=t_r[p][:], in0=t_r[p][:], scalar1=MAGIC, scalar2=None,
                    op0=ALU.subtract))

            @block.scalar
            def _(a, i=i, p=p):
                sig_toks[i] = ctx.ex(a, A, lambda: a.activation(
                    out=t_sig[p][:], in_=t_r[p][:], func=AF.Sigmoid,
                    scale=scb[:, 1:2]), deps=[r_toks[i], g2_toks[i - 2] if i >= 2 else None])

            @block.vector
            def _(v, i=i, p=p):
                ctx.ex(v, V, lambda: v.tensor_scalar(
                    out=t_sig[p][:], in0=t_sig[p][:], scalar1=65536.0,
                    scalar2=MAGIC, op0=ALU.mult, op1=ALU.add),
                    deps=[sig_toks[i]])
                ctx.ex(v, V, lambda: v.tensor_scalar(
                    out=t_sig[p][:], in0=t_sig[p][:], scalar1=MAGIC,
                    scalar2=65535.0, op0=ALU.subtract, op1=ALU.min))
                g2_toks[i] = ctx.ex(v, V, lambda: v.scalar_tensor_tensor(
                    out=gout[p][:], in0=t_r[p][:], scalar=scb[:, 2:3],
                    in1=t_sig[p][:], op0=ALU.mult, op1=ALU.mult),
                    deps=[st_toks[p] if i >= 2 else None])
                ctx.ex(v, V, lambda: v.tensor_reduce(
                    out=mxr[:], in_=gout[p][:], axis=AX, op=ALU.max,
                    apply_absolute_value=True))
                mx_tok[0] = ctx.ex(v, V, lambda: v.tensor_max(mx[:], mx[:], mxr[:]))

            @block.scalar
            def _(a, i=i, p=p):
                st_toks[p] = ctx.dma(a, f"stg{p}",
                                     gate2[i * 128:(i + 1) * 128, :], gout[p][:],
                                     deps=[g2_toks[i]])

        emit_colmax_to_out(ctx, block, mx, ident, m4loc, 1, mx_tok[0], "m4", ident_tok=ctx.sem("ld0").tok())

        @block.sync
        def _(sync):
            sync.wait_ge(ctx.sem("stg0").h, ctx.sem("stg0").n)
            sync.wait_ge(ctx.sem("stg1").h, ctx.sem("stg1").n)
            sync.wait_ge(ctx.sem("st_m4").h, ctx.sem("st_m4").n)

    return ctx.nc


def build_L4():
    ctx = Ctx("L4")
    nc = ctx.nc
    g2 = ctx.dram("gate2", [FF, NTOK], kind="ExternalInput")
    up = ctx.dram("up", [FF, NTOK], kind="ExternalInput")
    m2all = ctx.dram("m2all", [1, 8], kind="ExternalInput")
    m4all = ctx.dram("m4all", [1, 8], kind="ExternalInput")
    c_ident = ctx.dram("c_ident", [128, 128], kind="ExternalInput")
    o_out = ctx.dram("o_out", [FF, NTOK], kind="ExternalOutput")
    m5loc = ctx.dram("m5loc", [1, 1], kind="ExternalOutput")

    n_rows = FF // 128

    with ctx.es:
        sm2 = ctx.sbuf("sm2", [1, 8], F32)
        sm4 = ctx.sbuf("sm4", [1, 8], F32)
        ones = ctx.sbuf("ones", [1, 128], F32)
        ident = ctx.sbuf("ident", [128, 128], F32)
        scb2 = ctx.sbuf("scb2", [128, 8], F32)
        scb4 = ctx.sbuf("scb4", [128, 8], F32)
        g2in = [ctx.sbuf(f"g2in{i}", [128, NTOK], F32) for i in range(2)]
        upin = [ctx.sbuf(f"upin{i}", [128, NTOK], F32) for i in range(2)]
        t_g = [ctx.sbuf(f"t_g{i}", [128, NTOK], F32) for i in range(2)]
        t_u = [ctx.sbuf(f"t_u{i}", [128, NTOK], F32) for i in range(2)]
        oout = [ctx.sbuf(f"oout{i}", [128, NTOK], F32) for i in range(2)]
        mx = ctx.sbuf("mx", [128, 1], F32)
        mxr = ctx.sbuf("mxr", [128, 1], F32)
        ps = ctx.psum("ps", [128, 16], F32)

        block = ctx.es.enter_context(nc.Block())

        @block.sync
        def _(sync):
            ctx.dma(sync, "ld0", sm2[:], m2all)
            ctx.dma(sync, "ld0", sm4[:], m4all)
            ctx.dma(sync, "ld0", ident[:], c_ident)

        @block.vector
        def _(v):
            ctx.ex(v, V, lambda: v.memset(ones[:], 1.0))
            ctx.ex(v, V, lambda: v.memset(mx[:], 0.0))

        capd, cst_tok = emit_consts(ctx, block, [MAGIC, -MAGIC])
        t2 = emit_scalar_prep(ctx, block, sm2, 8, [], scb2, ps[0:128, 0:8],
                              ones, ctx.sem("ld0").tok())
        t4 = emit_scalar_prep(ctx, block, sm4, 8, [], scb4, ps[0:128, 8:16],
                              ones, ctx.sem("ld0").tok())

        ldg = [None, None]
        ldu = [None, None]
        o_toks = [None] * n_rows
        uq_toks = [None] * n_rows
        gq_toks = [None] * n_rows
        st_toks = [None, None]
        mx_tok = [None]

        for i in range(n_rows):
            p = i % 2

            @block.sync
            def _(sync, i=i, p=p):
                deps = [gq_toks[i - 2] if i >= 2 else None]
                ldg[p] = ctx.dma(sync, f"ldg{p}", g2in[p][:],
                                 g2[i * 128:(i + 1) * 128, :], deps=deps)

            @block.scalar
            def _(a, i=i, p=p):
                deps = [uq_toks[i - 2] if i >= 2 else None]
                ldu[p] = ctx.dma(a, f"ldu{p}", upin[p][:],
                                 up[i * 128:(i + 1) * 128, :], deps=deps)
                ctx.ex(a, A, lambda: a.activation(
                    out=t_u[p][:], in_=upin[p][:], func=AF.Identity,
                    scale=scb2[:, 0:1], bias=capd[MAGIC]),
                    deps=[ldu[p], t2, cst_tok,
                          o_toks[i - 2] if i >= 2 else None])
                ctx.ex(a, A, lambda: a.activation(
                    out=t_u[p][:], in_=t_u[p][:], func=AF.Identity,
                    bias=capd[-MAGIC]))
                uq_toks[i] = ctx.ex(a, A, lambda: a.activation(
                    out=t_u[p][:], in_=t_u[p][:], func=AF.Identity,
                    scale=scb2[:, 1:2]))

            @block.vector
            def _(v, i=i, p=p):
                ctx.ex(v, V, lambda: v.tensor_scalar(
                    out=t_g[p][:], in0=g2in[p][:], scalar1=scb4[:, 0:1],
                    scalar2=MAGIC, op0=ALU.mult, op1=ALU.add),
                    deps=[ldg[p], t4])
                gq_toks[i] = ctx.ex(v, V, lambda: v.tensor_scalar(
                    out=t_g[p][:], in0=t_g[p][:], scalar1=MAGIC, scalar2=None,
                    op0=ALU.subtract))
                o_toks[i] = ctx.ex(v, V, lambda: v.scalar_tensor_tensor(
                    out=oout[p][:], in0=t_g[p][:], scalar=scb4[:, 1:2],
                    in1=t_u[p][:], op0=ALU.mult, op1=ALU.mult),
                    deps=[uq_toks[i], st_toks[p] if i >= 2 else None])
                ctx.ex(v, V, lambda: v.tensor_reduce(
                    out=mxr[:], in_=oout[p][:], axis=AX, op=ALU.max,
                    apply_absolute_value=True))
                mx_tok[0] = ctx.ex(v, V, lambda: v.tensor_max(mx[:], mx[:], mxr[:]))

            @block.sync
            def _(sync, i=i, p=p):
                st_toks[p] = ctx.dma(sync, f"stg{p}",
                                     o_out[i * 128:(i + 1) * 128, :], oout[p][:],
                                     deps=[o_toks[i]])

        emit_colmax_to_out(ctx, block, mx, ident, m5loc, 1, mx_tok[0], "m5",
                           ident_tok=ctx.sem("ld0").tok())

        @block.sync
        def _(sync):
            sync.wait_ge(ctx.sem("stg0").h, ctx.sem("stg0").n)
            sync.wait_ge(ctx.sem("stg1").h, ctx.sem("stg1").n)
            sync.wait_ge(ctx.sem("st_m5").h, ctx.sem("st_m5").n)

    return ctx.nc


def emit_ln_tile(ctx, block, xt, xnb, st6, mv, sd, dep_ld, extra_deps=()):
    toks = {}

    @block.vector
    def _(v):
        ctx.ex(v, V, lambda: v.bn_stats(st6[:, 0:6], xt[:, 0:512]),
               deps=[dep_ld, *extra_deps])
        ctx.ex(v, V, lambda: v.bn_stats(st6[:, 6:12], xt[:, 512:1024]))
        toks["mv"] = ctx.ex(v, V, lambda: v.bn_aggr(mv[:], st6[:]))

    @block.scalar
    def _(a):
        toks["sd"] = ctx.ex(a, A, lambda: a.activation(
            out=sd[:, 0:1], in_=mv[:, 1:2], func=AF.Sqrt, bias=EPS),
            deps=[toks["mv"]])

    @block.vector
    def _(v):
        ctx.ex(v, V, lambda: v.reciprocal(sd[:, 1:2], sd[:, 0:1]),
               deps=[toks["sd"]])
        toks["xn"] = ctx.ex(v, V, lambda: v.tensor_scalar(
            out=xnb[:], in0=xt[:], scalar1=mv[:, 0:1], scalar2=sd[:, 1:2],
            op0=ALU.subtract, op1=ALU.mult))
    return toks["xn"]


def emit_ln_tile(ctx, block, xt, xnb, st6, mv, sd, dep_ld, carry, eps_ap,
                 extra_deps=()):
    toks = {}

    @block.vector
    def _(v):
        ctx.ex(v, V, lambda: v.bn_stats(st6[:, 0:6], xt[:, 0:512]),
               deps=[dep_ld, *extra_deps])
        ctx.ex(v, V, lambda: v.bn_stats(st6[:, 6:12], xt[:, 512:1024]))
        toks["mv"] = ctx.ex(v, V, lambda: v.bn_aggr(mv[:], st6[:]),
                            deps=[carry.get("sd")])

    @block.scalar
    def _(a):
        toks["sd"] = ctx.ex(a, A, lambda: a.activation(
            out=sd[:, 0:1], in_=mv[:, 1:2], func=AF.Sqrt, bias=eps_ap),
            deps=[toks["mv"], carry.get("recip"), carry.get("cst")])

    @block.vector
    def _(v):
        carry["recip"] = ctx.ex(v, V, lambda: v.reciprocal(sd[:, 1:2], sd[:, 0:1]),
                                deps=[toks["sd"]])
        toks["xn"] = ctx.ex(v, V, lambda: v.tensor_scalar(
            out=xnb[:], in0=xt[:], scalar1=mv[:, 0:1], scalar2=sd[:, 1:2],
            op0=ALU.subtract, op1=ALU.mult), deps=[carry.get("xnb_rd")])
    carry["sd"] = toks["sd"]
    return toks["xn"]


def emit_weight_dequant(ctx, block, tag, w_nat, O, I, wq_dst, bufs,
                        entry_deps=(), carry=None):
    n_nat = O // 128
    nblk = I // 32
    natb, amb, wqn = bufs["natb"], bufs["amb"], bufs["wqn"]
    carry = carry if carry is not None else {}
    nat_rd = carry.get("nat_rd", [None, None])
    nat_ld = [None, None]
    xp_tok = carry.get("xp_tok", [None, None])
    for t in range(n_nat):
        p = t % 2

        @block.scalar
        def _(a, t=t, p=p):
            deps = [nat_rd[p]] + (list(entry_deps) if t < 2 else [])
            nat_ld[p] = ctx.dma(a, f"{tag}nat{p}", natb[p][:, 0:I],
                                w_nat[t * 128:(t + 1) * 128, :], deps=deps)

        @block.vector
        def _(v, t=t, p=p):
            am = amb[:, p * nblk:(p + 1) * nblk]
            ctx.ex(v, V, lambda: v.reduce_max(
                am, natb[p][:, 0:I].rearrange("p (b e) -> p b e", e=32),
                axis=AX, apply_absolute_value=True), deps=[nat_ld[p]])
            ctx.ex(v, V, lambda: v.tensor_scalar(
                out=am, in0=am,
                scalar1=float(np.float32(1.0) / np.float32(7.0)),
                scalar2=1e-12, op0=ALU.mult, op1=ALU.max))
            ctx.ex(v, V, lambda: v.reciprocal(amb[:, 2 * nblk:2 * nblk + nblk],
                                              am))
            ctx.ex(v, V, lambda: v.tensor_tensor(
                out=natb[p][:, 0:I].rearrange("p (b e) -> p b e", e=32),
                in0=natb[p][:, 0:I].rearrange("p (b e) -> p b e", e=32),
                in1=amb[:, 2 * nblk:2 * nblk + nblk].unsqueeze(2)
                    .broadcast_to([128, nblk, 32]),
                op=ALU.mult))
            ctx.ex(v, V, lambda: v.tensor_scalar(
                out=natb[p][:, 0:I], in0=natb[p][:, 0:I], scalar1=MAGIC,
                scalar2=MAGIC, op0=ALU.add, op1=ALU.subtract),
                deps=[xp_tok[p]])
            nat_rd[p] = ctx.ex(v, V, lambda: v.tensor_tensor(
                out=wqn[p][:, 0:I].rearrange("p (b e) -> p b e", e=32),
                in0=natb[p][:, 0:I].rearrange("p (b e) -> p b e", e=32),
                in1=am.unsqueeze(2).broadcast_to([128, nblk, 32]),
                op=ALU.mult))

        @block.sync
        def _(sync, t=t, p=p):
            outv = wq_dst[:].rearrange("p (k o) -> p k o", o=O)[
                :, :, t * 128:(t + 1) * 128]
            inv = wqn[p][:, 0:I].rearrange("p (k c) -> p k c", c=128)
            xp_tok[p] = ctx.dma(sync, f"{tag}xp{p}", outv, inv,
                                deps=[nat_rd[p]], transpose=True)
    carry["nat_rd"], carry["xp_tok"] = nat_rd, xp_tok
    return xp_tok[(n_nat - 1) % 2], xp_tok[n_nat % 2]


def emit_mm_fm(ctx, block, tag, wq, rhs_fm, n_k, out_dram, n_m, mxcol, mxr,
               outb, ps_mm, wq_tok, rhs_toks, O, carry=None):
    n_tt = NTOK // 512
    carry = carry or {}
    evac = carry.get("evac", [None, None])
    st = carry.get("st", [None, None])
    red = carry.get("red", [None, None])
    mx_tok = [None]
    first_mx = [True]
    gi = 0
    for m in range(n_m):
        for tt in range(n_tt):
            p = gi % 2

            @block.tensor
            def _(t_, m=m, tt=tt, p=p, gi=gi):
                deps = ([wq_tok] + list(rhs_toks) if gi == 0 else []) + [evac[p]]
                for d in deps:
                    if d is not None:
                        t_.wait_ge(d[0].h, d[1])
                last = None
                for k in range(n_k):
                    last = t_.matmul(
                        ps_mm[p][:, 0:512],
                        wq[:, k * O + m * 128:k * O + (m + 1) * 128],
                        rhs_fm[:, k * NTOK + tt * 512:k * NTOK + (tt + 1) * 512],
                        start=(k == 0), stop=(k == n_k - 1))
                ctx.sem(f"{tag}mm").inc(last)

            @block.vector
            def _(v, m=m, tt=tt, p=p):
                ctx.ex(v, V, lambda: v.tensor_copy(outb[p][:], ps_mm[p][:, 0:512]),
                       deps=[ctx.sem(f"{tag}mm").tok(), st[p], red[p]])
                evac[p] = ctx.sem(V).tok()

            @block.vector
            def _(v, m=m, tt=tt, p=p):
                red[p] = ctx.ex(v, V, lambda: v.tensor_reduce(
                    out=mxr[:], in_=outb[p][:], axis=AX, op=ALU.max,
                    apply_absolute_value=True), deps=[evac[p]])
                if first_mx[0]:
                    mx_tok[0] = ctx.ex(v, V, lambda: v.tensor_copy(mxcol[:], mxr[:]))
                    first_mx[0] = False
                else:
                    mx_tok[0] = ctx.ex(v, V, lambda: v.tensor_max(
                        mxcol[:], mxcol[:], mxr[:]))

            @block.sync
            def _(sync, m=m, tt=tt, p=p):
                st[p] = ctx.dma(
                    sync, f"{tag}st{p}",
                    out_dram[m * 128:(m + 1) * 128, tt * 512:(tt + 1) * 512],
                    outb[p][:], deps=[evac[p]])
            gi += 1
    carry["evac"], carry["st"], carry["red"] = evac, st, red
    return st, mx_tok[0], carry


def build_L2():
    ctx = Ctx("L2")
    nc = ctx.nc
    x = ctx.dram("x", [NTOK, D], kind="ExternalInput")
    m1all = ctx.dram("m1all", [1, 8], kind="ExternalInput")
    wup_nat = ctx.dram("wup_nat", [FF, D], kind="ExternalInput")
    wgate_nat = ctx.dram("wgate_nat", [FF, D], kind="ExternalInput")
    c_ident = ctx.dram("c_ident", [128, 128], kind="ExternalInput")
    up_o = ctx.dram("up", [FF, NTOK], kind="ExternalOutput")
    gate_o = ctx.dram("gate", [FF, NTOK], kind="ExternalOutput")
    m23loc = ctx.dram("m23loc", [1, 2], kind="ExternalOutput")

    n_ttiles = NTOK // 128

    with ctx.es:
        sm1 = ctx.sbuf("sm1", [1, 8], F32)
        ones = ctx.sbuf("ones", [1, 128], F32)
        ident = ctx.sbuf("ident", [128, 128], F32)
        scb1 = ctx.sbuf("scb1", [128, 8], F32)
        st6 = ctx.sbuf("st6", [128, 12], F32)
        mv = ctx.sbuf("mv", [128, 2], F32)
        sd = ctx.sbuf("sd", [128, 2], F32)
        xqb = [ctx.sbuf(f"xqb{i}", [128, D], BF16) for i in range(2)]
        xq_fm = ctx.sbuf("xq_fm", [128, 8 * NTOK], BF16, is_transpose_dest=True)
        natb = [ctx.sbuf(f"natb{i}", [128, D], F32) for i in range(2)]
        xin = [ctx.sbuf(f"xin{i}", [128, D], F32) for i in range(2)]
        xnb = ctx.sbuf("xnb", [128, D], F32)
        amb = ctx.sbuf("amb", [128, 3 * 32], F32)
        wqn = [ctx.sbuf(f"wqn{i}", [128, D], BF16, is_transpose_dest=False)
               for i in range(2)]
        wq2 = [ctx.sbuf(f"wq2_{i}", [128, 8 * FF], BF16, is_transpose_dest=True)
               for i in range(2)]
        outb = [ctx.sbuf(f"outb{i}", [128, 512], F32) for i in range(2)]
        mxr = ctx.sbuf("mxr", [128, 1], F32)
        mx23 = ctx.sbuf("mx23", [128, 2], F32)
        ps_sc = ctx.psum("ps_sc", [128, 128], F32)
        ps_mm0 = ctx.psum("ps_mm0", [128, 512], F32)
        ps_mm1 = ctx.psum("ps_mm1", [128, 512], F32)

        block = ctx.es.enter_context(nc.Block())

        @block.sync
        def _(sync):
            ctx.dma(sync, "ld0", sm1[:], m1all)
            ctx.dma(sync, "ld0", ident[:], c_ident)
        ld0 = ctx.sem("ld0").tok()

        @block.vector
        def _(v):
            ctx.ex(v, V, lambda: v.memset(ones[:], 1.0))

        capd, cst_tok = emit_consts(ctx, block, [EPS])
        scb_tok = emit_scalar_prep(ctx, block, sm1, 8, [], scb1,
                                   ps_sc[0:128, 0:8], ones, ld0)

        x_ld = [None, None]
        xq_rd = [None, None]
        xp_sem = [ctx.sem("xp0"), ctx.sem("xp1")]
        for t in range(n_ttiles):
            p = t % 2

            @block.sync
            def _(sync, t=t, p=p):
                x_ld[p] = ctx.dma(sync, f"xld{p}", xin[p][:],
                                  x[t * 128:(t + 1) * 128, :],
                                  deps=[xq_rd[p]])

            if t == 0:
                ln_carry = {"cst": cst_tok}
            xn_tok = emit_ln_tile(ctx, block, xin[p], xnb, st6, mv, sd,
                                  x_ld[p], ln_carry, capd[EPS])

            @block.vector
            def _(v, t=t, p=p):
                ctx.ex(v, V, lambda: v.tensor_scalar(
                    out=xnb[:], in0=xnb[:], scalar1=scb1[:, 0:1], scalar2=MAGIC,
                    op0=ALU.mult, op1=ALU.add),
                    deps=[scb_tok, xp_sem[p].tok()])
                xq_rd[p] = ctx.ex(v, V, lambda: v.tensor_scalar(
                    out=xqb[p][:], in0=xnb[:], scalar1=MAGIC, scalar2=scb1[:, 1:2],
                    op0=ALU.subtract, op1=ALU.mult))

            @block.scalar
            def _(a, t=t, p=p):
                outv = xq_fm[:].rearrange("p (k q) -> p k q", q=NTOK)[
                    :, :, t * 128:(t + 1) * 128]
                inv = xqb[p][:].rearrange("p (k c) -> p k c", c=128)
                ctx.dma(a, f"xp{p}", outv, inv, deps=[xq_rd[p]],
                        transpose=True)
        xq_all = (ctx.sem("xp0"), ctx.sem("xp0").n)
        xq_all1 = (ctx.sem("xp1"), ctx.sem("xp1").n)

        bufs = dict(natb=natb, amb=amb, wqn=wqn)
        dq_carry = {}

        uq1, uq2 = emit_weight_dequant(ctx, block, "u", wup_nat, FF, D,
                                       wq2[0], bufs, carry=dq_carry)
        st_u, mx_u, mm_carry = emit_mm_fm(
            ctx, block, "u", wq2[0], xq_fm, D // 128, up_o, FF // 128,
            mx23[:, 0:1], mxr, outb, [ps_mm0, ps_mm1], uq1,
            [uq2, xq_all, xq_all1], FF)

        gq1, gq2 = emit_weight_dequant(ctx, block, "g", wgate_nat, FF, D,
                                       wq2[1], bufs, carry=dq_carry)
        st_g, mx_g, _ = emit_mm_fm(
            ctx, block, "g", wq2[1], xq_fm, D // 128, gate_o, FF // 128,
            mx23[:, 1:2], mxr, outb, [ps_mm0, ps_mm1], gq1, [gq2],
            FF, carry=mm_carry)

        emit_colmax_to_out(ctx, block, mx23, ident, m23loc, 2, mx_g, "m23",
                           ident_tok=ld0)

        @block.sync
        def _(sync):
            for s in ("ust0", "ust1", "gst0", "gst1", "st_m23"):
                sync.wait_ge(ctx.sem(s).h, ctx.sem(s).n)

    return ctx.nc


def build_L5():
    ctx = Ctx("L5")
    nc = ctx.nc
    o_in = ctx.dram("o_in", [FF, NTOK], kind="ExternalInput")
    m5all = ctx.dram("m5all", [1, 8], kind="ExternalInput")
    wdown_nat = ctx.dram("wdown_nat", [D, FF], kind="ExternalInput")
    partial = ctx.dram("partial", [NTOK, D], kind="ExternalInput")
    c_ident = ctx.dram("c_ident", [128, 128], kind="ExternalInput")
    outp = ctx.dram("outp", [NTOK, D], kind="ExternalOutput")

    QT = 1024
    n_q = NTOK // QT
    n_k = FF // 128

    with ctx.es:
        sm5 = ctx.sbuf("sm5", [1, 8], F32)
        ones = ctx.sbuf("ones", [1, 128], F32)
        ident = ctx.sbuf("ident", [128, 128], F32)
        scb5 = ctx.sbuf("scb5", [128, 8], F32)
        natb = [ctx.sbuf(f"natb{i}", [128, FF], F32) for i in range(2)]
        amb = ctx.sbuf("amb", [128, 3 * 96], F32)
        wqn = [ctx.sbuf(f"wqn{i}", [128, FF], BF16) for i in range(2)]
        wq = ctx.sbuf("wq", [128, n_k * D], BF16, is_transpose_dest=True)
        oq = ctx.sbuf("oq", [128, n_k * QT], BF16)
        oin = [ctx.sbuf(f"oin{i}", [128, QT], F32) for i in range(2)]
        padd = [ctx.sbuf(f"padd{i}", [128, D], F32) for i in range(2)]
        outb = [ctx.sbuf(f"outb{i}", [128, D], F32) for i in range(2)]
        ps_all = ctx.psum("ps_all", [128, 4096], F32)

        block = ctx.es.enter_context(nc.Block())

        @block.sync
        def _(sync):
            ctx.dma(sync, "ld0", sm5[:], m5all)
            ctx.dma(sync, "ld0", ident[:], c_ident)
        ld0 = ctx.sem("ld0").tok()

        @block.vector
        def _(v):
            ctx.ex(v, V, lambda: v.memset(ones[:], 1.0))

        scb_tok = emit_scalar_prep(ctx, block, sm5, 8, [], scb5,
                                   ps_all[0:128, 0:8], ones, ld0)

        bufs = dict(natb=natb, amb=amb, wqn=wqn)
        wq_tok, wq_tok2 = emit_weight_dequant(ctx, block, "d", wdown_nat,
                                              D, FF, wq, bufs)

        o_ld = [None, None]
        o_rd = [None, None]
        oq_toks = [None] * n_k
        mm_done = [None]
        evac_done = [None] * 4
        pa_ld = [None, None]
        pa_rd = [None, None]
        st = [None, None]

        for q in range(n_q):
            for k in range(n_k):
                p = k % 2

                @block.sync
                def _(sync, q=q, k=k, p=p):
                    deps = [o_rd[p], mm_done[0] if k < 2 else None]
                    o_ld[p] = ctx.dma(sync, f"old{p}", oin[p][:],
                                      o_in[k * 128:(k + 1) * 128,
                                           q * QT:(q + 1) * QT], deps=deps)

                @block.vector
                def _(v, q=q, k=k, p=p):
                    ctx.ex(v, V, lambda: v.tensor_scalar(
                        out=oin[p][:], in0=oin[p][:], scalar1=scb5[:, 0:1],
                        scalar2=MAGIC, op0=ALU.mult, op1=ALU.add),
                        deps=[o_ld[p], scb_tok if (q == 0 and k < 2) else None])
                    o_rd[p] = oq_toks[k] = ctx.ex(v, V, lambda: v.tensor_scalar(
                        out=oq[:, k * QT:(k + 1) * QT], in0=oin[p][:],
                        scalar1=MAGIC, scalar2=scb5[:, 1:2],
                        op0=ALU.subtract, op1=ALU.mult))

            for g in range(2):
                @block.tensor
                def _(t_, q=q, g=g):
                    deps = [oq_toks[n_k - 1],
                            wq_tok if q == 0 and g == 0 else None,
                            wq_tok2 if q == 0 and g == 0 else None]
                    deps += evac_done
                    for d in deps:
                        if d is not None:
                            t_.wait_ge(d[0].h, d[1])
                    last = None
                    for k in range(n_k):
                        for c in range(4):
                            lhs = oq[:, k * QT + g * 512 + c * 128:
                                     k * QT + g * 512 + (c + 1) * 128]
                            for nn in range(2):
                                last = t_.matmul(
                                    ps_all[:, c * 1024 + nn * 512:
                                           c * 1024 + (nn + 1) * 512],
                                    lhs,
                                    wq[:, k * D + nn * 512:k * D + (nn + 1) * 512],
                                    start=(k == 0), stop=(k == n_k - 1))
                    mm_done[0] = ctx.sem("fmm").inc(last)

                for c in range(4):
                    pc = c % 2
                    r0 = q * QT + g * 512 + c * 128

                    @block.scalar
                    def _(a, r0=r0, pc=pc, c=c):
                        pa_ld[pc] = ctx.dma(a, f"pld{pc}", padd[pc][:],
                                            partial[r0:r0 + 128, :],
                                            deps=[pa_rd[pc]])

                    @block.vector
                    def _(v, r0=r0, pc=pc, c=c):
                        pa_rd[pc] = evac_done[c] = ctx.ex(
                            v, V, lambda: v.tensor_tensor(
                                out=outb[pc][:], in0=padd[pc][:],
                                in1=ps_all[:, c * 1024:(c + 1) * 1024],
                                op=ALU.add),
                            deps=[pa_ld[pc], mm_done[0], st[pc]])

                    @block.sync
                    def _(sync, r0=r0, pc=pc):
                        st[pc] = ctx.dma(sync, f"ost{pc}", outp[r0:r0 + 128, :],
                                         outb[pc][:],
                                         deps=[evac_done[c] if True else None])

        @block.sync
        def _(sync):
            for s in ("ost0", "ost1"):
                sync.wait_ge(ctx.sem(s).h, ctx.sem(s).n)

    return ctx.nc


def build_L1():
    ctx = Ctx("L1")
    nc = ctx.nc
    x_in = ctx.dram("x", [NTOK, D], kind="ExternalInput")
    w_inT = ctx.dram("w_inT", [D, 2 * INNER], kind="ExternalInput")
    w_dtT = ctx.dram("w_dtT", [INNER, INNER], kind="ExternalInput")
    w_outT = ctx.dram("w_outT", [INNER, D], kind="ExternalInput")
    w_norm = ctx.dram("w_norm", [1, D], kind="ExternalInput")
    b_dt = ctx.dram("b_dt", [1, INNER], kind="ExternalInput")
    c_ident = ctx.dram("c_ident", [128, 128], kind="ExternalInput")
    partial = ctx.dram("partial", [NTOK, D], kind="ExternalOutput")
    m1loc = ctx.dram("m1loc", [1, 1], kind="ExternalOutput")
    s2_dram = ctx.dram("s2_dram", [2, INNER, SEQ], BF16, kind="Internal")

    NT = SEQ // 128
    NKD = D // 128
    NJ = INNER // 128
    with ctx.es:
        ident = ctx.sbuf("ident", [128, 128], F32)
        ones = ctx.sbuf("ones", [1, 128], F32)
        wnb = ctx.sbuf("wnb", [128, D], F32)
        bdt = ctx.sbuf("bdt", [128, 16], F32)
        st6 = ctx.sbuf("st6", [128, 12], F32)
        mv = [ctx.sbuf(f"mv{i}", [128, 2], F32) for i in range(2)]
        rs2 = [ctx.sbuf(f"rs2{i}", [128, 4], F32) for i in range(2)]
        mx = ctx.sbuf("mx", [128, 1], F32)
        mxr = ctx.sbuf("mxr", [128, 1], F32)
        hbf = [ctx.sbuf(f"hbf{i}", [128, D], BF16) for i in range(2)]
        win = ctx.sbuf("win", [128, NKD * INNER], BF16)
        us = ctx.sbuf("us", [128, NJ * SEQ], BF16)
        h_fm = ctx.sbuf("h_fm", [128, NKD * SEQ], BF16, is_transpose_dest=True)
        wdtb = [ctx.sbuf(f"wdtb{i}", [128, NJ * 128], BF16) for i in range(2)]
        lam = ctx.sbuf("lam", [128, SEQ], F32)
        siluz = ctx.sbuf("siluz", [128, SEQ], F32)
        bneg = ctx.sbuf("bneg", [128, SEQ], F32)
        sneg = ctx.sbuf("sneg", [128, SEQ], F32)
        s2b = [ctx.sbuf(f"s2b{i}", [128, SEQ], BF16) for i in range(2)]
        s2k = [ctx.sbuf(f"s2k{i}", [128, 512], BF16) for i in range(2)]
        ps_all = ctx.psum("ps_all", [128, 4096], F32)

        xbuf = [ctx.sbuf(f"xbuf{i}", [128, D], F32) for i in range(2)]
        wn_row = xbuf[0][0:1, 0:D]
        bdt_raw = xbuf[1][0:16, 0:128]
        xnb = ctx.sbuf("xnb", [128, D], F32)
        sq = [ctx.sbuf(f"sqb{i}", [128, D], F32) for i in range(2)]
        padd = [bneg[:, 0:D], bneg[:, D:2 * D]]
        outb = [sneg[:, 0:D], sneg[:, D:2 * D]]
        wob = us[:, 0:NJ * D]

        block = ctx.es.enter_context(nc.Block())

        @block.sync
        def _(sync):
            ctx.dma(sync, "ld0", ident[:], c_ident)
            ctx.dma(sync, "ld0", wn_row[:], w_norm)
            ctx.dma(sync, "ld0", bdt_raw[:],
                    b_dt.rearrange("o (j p) -> (o j) p", p=128))
        ld0 = ctx.sem("ld0").tok()

        capd, cst_tok = emit_consts(ctx, block, [EPS])

        @block.vector
        def _(v):
            ctx.ex(v, V, lambda: v.memset(ones[:], 1.0))
            ctx.ex(v, V, lambda: v.memset(mx[:], 0.0))

        @block.tensor
        def _(t):
            ctx.ex(t, T, lambda: t.matmul(ps_all[:, 0:512], ones[0:1, :],
                                          wn_row[0:1, 0:512], start=True,
                                          stop=True),
                   deps=[ld0, ctx.sem(V).tok()])
            ctx.ex(t, T, lambda: t.matmul(ps_all[:, 512:1024], ones[0:1, :],
                                          wn_row[0:1, 512:1024], start=True,
                                          stop=True))
            ctx.ex(t, T, lambda: t.transpose(ps_all[0:128, 1024:1040],
                                             bdt_raw[0:16, :], ident[0:16, 0:16]))

        @block.scalar
        def _(a):
            ctx.ex(a, A, lambda: a.copy(wnb[:], ps_all[:, 0:1024]),
                   deps=[ctx.sem(T).tok()])
            ctx.ex(a, A, lambda: a.copy(bdt[:], ps_all[0:128, 1024:1040]))
        prep_tok = ctx.sem(A).tok()

        pstore = [[None, None], [None, None]]
        pstore_all = [[None] * NT, [None] * NT]
        hxp_all_toks = [[None] * NT, [None] * NT]
        mx_tok = [None]
        cmm_z_done = [None]
        lam_mm_done = [None]
        fmm_done = [None]
        z_done = [None]
        scan_tok = [None]
        s2_tok = [None]
        win_rd = [None]
        s2_ld = [None, None]
        s2_rd = [None, None]
        ev_done = [None] * 4
        pa_ld = [None, None]
        pa_rd = [None, None]
        ost = [None, None]
        ln_carry = {"cst": cst_tok}

        for s in range(2):
            x_ld = [None, None]
            x_rd = [None, None]
            hxp = [None, None]
            stats_tok = [None, None]
            sqrt_tok = [None, None]
            fin_tok = [None, None]

            def emit_stats(t):
                p = t % 2
                row0 = s * SEQ + t * 128

                @block.sync
                def _(sync, t=t, p=p, row0=row0):
                    deps = [x_rd[p], scan_tok[0] if t < 2 else None,
                            prep_tok if (s == 0 and t < 2) else None]
                    x_ld[p] = ctx.dma(sync, f"xld{p}", xbuf[p][:],
                                      x_in[row0:row0 + 128, :], deps=deps)

                @block.vector
                def _(v, t=t, p=p):
                    ctx.ex(v, V, lambda: v.bn_stats(st6[:, 0:6],
                                                    xbuf[p][:, 0:512]),
                           deps=[x_ld[p]])
                    ctx.ex(v, V, lambda: v.bn_stats(st6[:, 6:12],
                                                    xbuf[p][:, 512:1024]))
                    ctx.ex(v, V, lambda: v.bn_aggr(mv[p][:], st6[:]),
                           deps=[sqrt_tok[p]])
                    ctx.ex(v, V, lambda: v.tensor_scalar(
                        out=rs2[p][:, 1:2], in0=mv[p][:, 1:2], scalar1=1.0,
                        scalar2=EPS, op0=ALU.mult, op1=ALU.add))
                    ctx.ex(v, V, lambda: v.reciprocal(rs2[p][:, 0:1],
                                                      rs2[p][:, 1:2]))
                    ctx.ex(v, V, lambda: v.tensor_tensor(
                        out=rs2[p][:, 0:1], in0=rs2[p][:, 0:1],
                        in1=mv[p][:, 1:2], op=ALU.mult))
                    stats_tok[p] = ctx.ex(v, V, lambda: v.tensor_scalar(
                        out=rs2[p][:, 0:1], in0=rs2[p][:, 0:1], scalar1=1.0,
                        scalar2=EPS, op0=ALU.mult, op1=ALU.add))

                @block.scalar
                def _(a, t=t, p=p):
                    sqrt_tok[p] = ctx.ex(a, A, lambda: a.activation(
                        out=rs2[p][:, 2:4], in_=rs2[p][:, 0:2], func=AF.Sqrt),
                        deps=[stats_tok[p], fin_tok[p]])

            def emit_finish(t):
                p = t % 2
                row0 = s * SEQ + t * 128

                @block.vector
                def _(v, t=t, p=p):
                    ctx.ex(v, V, lambda: v.reciprocal(rs2[p][:, 0:1],
                                                      rs2[p][:, 2:3]),
                           deps=[sqrt_tok[p]])
                    ctx.ex(v, V, lambda: v.reciprocal(rs2[p][:, 1:2],
                                                      rs2[p][:, 3:4]))
                    ctx.ex(v, V, lambda: v.tensor_scalar(
                        out=xnb[:], in0=xbuf[p][:], scalar1=mv[p][:, 0:1],
                        scalar2=rs2[p][:, 1:2], op0=ALU.subtract,
                        op1=ALU.mult))
                    ctx.ex(v, V, lambda: v.tensor_reduce(
                        out=mxr[:], in_=xnb[:], axis=AX, op=ALU.max,
                        apply_absolute_value=True))
                    mx_tok[0] = ctx.ex(v, V, lambda: v.tensor_max(
                        mx[:], mx[:], mxr[:]))
                    ctx.ex(v, V, lambda: v.scalar_tensor_tensor(
                        out=hbf[p][:], in0=xnb[:], scalar=rs2[p][:, 0:1],
                        in1=wnb[:], op0=ALU.mult, op1=ALU.mult),
                        deps=[prep_tok if (s == 0 and t == 0) else None,
                              hxp[p],
                              cmm_z_done[0] if t < 2 else None])
                    x_rd[p] = ctx.ex(v, V, lambda: v.tensor_tensor(
                        out=sq[p][:], in0=xbuf[p][:], in1=xnb[:], op=ALU.add),
                        deps=[pstore[s][p],
                              fmm_done[0] if t < 2 else None])
                    fin_tok[p] = x_rd[p]

                @block.scalar
                def _(a, t=t, p=p):
                    outv = h_fm[:].rearrange("p (k q) -> p k q", q=SEQ)[
                        :, :, t * 128:(t + 1) * 128]
                    inv = hbf[p][:].rearrange("p (k c) -> p k c", c=128)
                    hxp[p] = ctx.dma(a, f"hxp{p}", outv, inv,
                                     deps=[ctx.sem(V).tok()], transpose=True)
                    hxp_all_toks[s][t] = hxp[p]

                @block.sync
                def _(sync, t=t, p=p, row0=row0):
                    pstore[s][p] = pstore_all[s][t] = ctx.dma(
                        sync, f"pst{p}", partial[row0:row0 + 128, :],
                        sq[p][:], deps=[x_rd[p]])

            emit_stats(0)
            for t in range(1, NT):
                emit_stats(t)
                emit_finish(t - 1)
            emit_finish(NT - 1)

            @block.gpsimd
            def _(g, s=s):
                for k in range(NKD):
                    ctx.dma(g, "wua", win[:, k * INNER:(k + 1) * INNER],
                            w_inT[k * 128:(k + 1) * 128, 0:INNER],
                            deps=[win_rd[0]] if k == 0 else ())
            wua_tok = ctx.sem("wua").tok()

            bmm_evac = [None, None]
            us_tok = [None]
            hxp_seq = [hxp_all_toks[s][NT - 1], hxp_all_toks[s][NT - 2]]
            for m in range(NJ):
                pb = m % 2

                @block.tensor
                def _(t_, s=s, m=m, pb=pb):
                    deps = ([wua_tok, ctx.sem("clmm").tok(),
                             ctx.sem("fpk").tok(), *hxp_seq]
                            if m == 0 else []) + [bmm_evac[pb]]
                    for d in deps:
                        if d is not None:
                            t_.wait_ge(d[0].h, d[1])
                    last = None
                    for tt in range(SEQ // 512):
                        base = pb * 2048 + tt * 512
                        for k in range(NKD):
                            last = t_.matmul(
                                ps_all[:, base:base + 512],
                                win[:, k * INNER + m * 128:k * INNER + (m + 1) * 128],
                                h_fm[:, k * SEQ + tt * 512:k * SEQ + (tt + 1) * 512],
                                start=(k == 0), stop=(k == NKD - 1))
                    ctx.sem("bmm").inc(last)

                @block.scalar
                def _(a, s=s, m=m, pb=pb):
                    ctx.ex(a, A, lambda: a.activation(
                        out=us[:, m * SEQ:(m + 1) * SEQ],
                        in_=ps_all[:, pb * 2048:(pb + 1) * 2048],
                        func=AF.Silu), deps=[ctx.sem("bmm").tok()])
                    bmm_evac[pb] = us_tok[0] = ctx.sem(A).tok()

            @block.gpsimd
            def _(g, s=s):
                g.wait_ge(ctx.sem("bmm").h, ctx.sem("bmm").n)
                for k in range(NKD):
                    ctx.dma(g, "wza", win[:, k * INNER:(k + 1) * INNER],
                            w_inT[k * 128:(k + 1) * 128, INNER:2 * INNER])
            wza_tok = ctx.sem("wza").tok()

            wdt_ld = [None, None]
            lam_evac = [None]
            z_evac = [None]
            s2st = [None, None]
            for j in range(NJ):
                pj = j % 2

                @block.gpsimd
                def _(g, s=s, j=j, pj=pj):
                    deps = [ctx.sem("clmm").tok() if j >= 2 else None]
                    wdt_ld[pj] = ctx.dma(
                        g, f"wdt{pj}",
                        wdtb[pj][:].rearrange("p (k j) -> p k j", j=128),
                        w_dtT[:, j * 128:(j + 1) * 128].rearrange(
                            "(k p) j -> p k j", p=128),
                        deps=deps)

                @block.tensor
                def _(t_, s=s, j=j, pj=pj):
                    deps = [wdt_ld[pj], lam_evac[0],
                            us_tok[0] if j == 0 else None]
                    for d in deps:
                        if d is not None:
                            t_.wait_ge(d[0].h, d[1])
                    last = None
                    for tt in range(SEQ // 512):
                        for k in range(NJ):
                            last = t_.matmul(
                                ps_all[:, tt * 512:(tt + 1) * 512],
                                wdtb[pj][:, k * 128:(k + 1) * 128],
                                us[:, k * SEQ + tt * 512:k * SEQ + (tt + 1) * 512],
                                start=(k == 0), stop=(k == NJ - 1))
                    ctx.sem("clmm").inc(last)
                    deps = [wza_tok if j == 0 else None, z_done[0]]
                    for d in deps:
                        if d is not None:
                            t_.wait_ge(d[0].h, d[1])
                    last = None
                    for tt in range(SEQ // 512):
                        for k in range(NKD):
                            last = t_.matmul(
                                ps_all[:, 2048 + tt * 512:2048 + (tt + 1) * 512],
                                win[:, k * INNER + j * 128:k * INNER + (j + 1) * 128],
                                h_fm[:, k * SEQ + tt * 512:k * SEQ + (tt + 1) * 512],
                                start=(k == 0), stop=(k == NKD - 1))
                    cmm_z_done[0] = ctx.sem("czmm").inc(last)

                @block.scalar
                def _(a, s=s, j=j, pj=pj):
                    ctx.ex(a, A, lambda: a.activation(
                        out=lam[:], in_=ps_all[:, 0:2048], func=AF.Sigmoid,
                        bias=bdt[:, j:j + 1]),
                        deps=[ctx.sem("clmm").tok(), scan_tok[0],
                              fmm_done[0] if j == 0 else None,
                              pa_rd[0] if j == 0 else None,
                              pa_rd[1] if j == 0 else None])
                    lam_evac[0] = ctx.sem(A).tok()
                    ctx.ex(a, A, lambda: a.activation(
                        out=siluz[:], in_=ps_all[:, 2048:4096], func=AF.Sigmoid),
                        deps=[ctx.sem("czmm").tok(), s2_tok[0],
                              ost[0] if j == 0 else None,
                              ost[1] if j == 0 else None])
                    z_evac[0] = ctx.sem(A).tok()

                @block.vector
                def _(v, s=s, j=j, pj=pj):
                    ctx.ex(v, V, lambda: v.scalar_tensor_tensor(
                        out=bneg[:], in0=lam[:], scalar=1.0,
                        in1=us[:, j * SEQ:(j + 1) * SEQ],
                        op0=ALU.subtract, op1=ALU.mult),
                        deps=[lam_evac[0],
                              pstore[s][0] if j == 0 else None,
                              pstore[s][1] if j == 0 else None,
                              x_ld[0] if j == 0 else None,
                              x_ld[1] if j == 0 else None])
                    scan_tok[0] = ctx.ex(v, V, lambda: v.tensor_tensor_scan(
                        out=sneg[:], data0=lam[:], data1=bneg[:], initial=0.0,
                        op0=ALU.mult, op1=ALU.add))
                    ctx.ex(v, V, lambda: v.tensor_tensor(
                        out=siluz[:], in0=siluz[:], in1=ps_all[:, 2048:4096],
                        op=ALU.mult), deps=[z_evac[0]])
                    z_done[0] = ctx.sem(V).tok()
                    s2_tok[0] = ctx.ex(v, V, lambda: v.scalar_tensor_tensor(
                        out=s2b[pj][:], in0=sneg[:], scalar=-1.0, in1=siluz[:],
                        op0=ALU.mult, op1=ALU.mult),
                        deps=[s2st[pj]])

                @block.sync
                def _(sync, s=s, j=j, pj=pj):
                    s2st[pj] = ctx.dma(sync, f"s2st{pj}",
                                       s2_dram[s, j * 128:(j + 1) * 128, :],
                                       s2b[pj][:], deps=[s2_tok[0]])

            @block.gpsimd
            def _(g, s=s):
                g.wait_ge(ctx.sem("clmm").h, ctx.sem("clmm").n)
                for k in range(NJ):
                    ctx.dma(g, "woa", wob[:, k * D:(k + 1) * D],
                            w_outT[k * 128:(k + 1) * 128, :])
            woa_tok = ctx.sem("woa").tok()

            for g_ in range(4):
                for k in range(NJ):
                    pk = k % 2

                    @block.sync
                    def _(sync, s=s, g_=g_, k=k, pk=pk):
                        deps = [s2_rd[pk],
                                s2st[0] if (g_ == 0 and k == 0) else None,
                                s2st[1] if (g_ == 0 and k == 0) else None]
                        s2_ld[pk] = ctx.dma(
                            sync, f"s2ld{pk}", s2k[pk][:],
                            s2_dram[s, k * 128:(k + 1) * 128,
                                    g_ * 512:(g_ + 1) * 512], deps=deps)

                    @block.tensor
                    def _(t_, s=s, g_=g_, k=k, pk=pk):
                        deps = [s2_ld[pk]] +                                ([woa_tok] if (g_ == 0 and k == 0) else []) +                                (ev_done if k == 0 else [])
                        for d in deps:
                            if d is not None:
                                t_.wait_ge(d[0].h, d[1])
                        last = None
                        for c in range(4):
                            for nn in range(2):
                                last = t_.matmul(
                                    ps_all[:, c * 1024 + nn * 512:
                                           c * 1024 + (nn + 1) * 512],
                                    s2k[pk][:, c * 128:(c + 1) * 128],
                                    wob[:, k * D + nn * 512:k * D + (nn + 1) * 512],
                                    start=(k == 0), stop=(k == NJ - 1))
                        s2_rd[pk] = ctx.sem("fpk").inc(last)
                        if k == NJ - 1:
                            fmm_done[0] = ctx.sem("fpk").tok()

                for c in range(4):
                    pc = c % 2
                    row0 = s * SEQ + g_ * 512 + c * 128
                    tile_idx = g_ * 4 + c

                    @block.scalar
                    def _(a, s=s, c=c, pc=pc, row0=row0, tile_idx=tile_idx):
                        pa_ld[pc] = ctx.dma(a, f"pld{pc}", padd[pc][:],
                                            partial[row0:row0 + 128, :],
                                            deps=[pa_rd[pc],
                                                  pstore_all[s][tile_idx],
                                                  lam_evac[0] if tile_idx < 2 else None,
                                                  z_evac[0] if tile_idx < 2 else None,
                                                  s2_tok[0] if tile_idx < 2 else None])

                    @block.vector
                    def _(v, s=s, c=c, pc=pc):
                        pa_rd[pc] = ev_done[c] = ctx.ex(
                            v, V, lambda: v.tensor_tensor(
                                out=outb[pc][:], in0=padd[pc][:],
                                in1=ps_all[:, c * 1024:(c + 1) * 1024],
                                op=ALU.add),
                            deps=[pa_ld[pc], ctx.sem("fpk").tok(), ost[pc]])

                    @block.sync
                    def _(sync, s=s, pc=pc, row0=row0):
                        ost[pc] = ctx.dma(sync, f"ost{pc}",
                                          partial[row0:row0 + 128, :],
                                          outb[pc][:], deps=[ev_done[c]])
            win_rd[0] = ctx.sem("czmm").tok()

        ctx.colmax_scratch = xnb
        emit_colmax_to_out(ctx, block, mx, ident, m1loc, 1, mx_tok[0], "m1",
                           ident_tok=ld0, ps_view=ps_all,
                           extra_dep=ctx.sem(V).tok())

        @block.sync
        def _(sync):
            for sname in ("ost0", "ost1", "st_m1", "pst0", "pst1"):
                sync.wait_ge(ctx.sem(sname).h, ctx.sem(sname).n)

    return ctx.nc


_CACHE = {}


def _get(name, builder):
    if name not in _CACHE:
        _CACHE[name] = builder()
    return _CACHE[name]


def _make_e(nblk, nchunk):
    e = np.zeros((nblk, nchunk * 128), np.float32)
    for c in range(nchunk):
        for p in range(128):
            e[4 * c + p // 32, c * 128 + p] = 1.0
    return e


def _run(name, builder, in_maps):
    nc = _get(name, builder)
    res = run_bass_kernel_spmd(nc, in_maps, core_ids=list(range(NCORES)))
    return res.results


def kernel(L0_attn_x, L0_attn_w_norm, L0_attn_W_in, L0_attn_W_dt, L0_attn_b_dt,
           L0_attn_W_out, L0_ffn_w_gate, L0_ffn_w_up, L0_ffn_w_down):
    f32c = lambda a: np.ascontiguousarray(np.asarray(a), dtype=np.float32)
    x = f32c(L0_attn_x)
    xs = [np.ascontiguousarray(x[2 * i:2 * i + 2].reshape(NTOK, D))
          for i in range(NCORES)]
    w_inT = np.ascontiguousarray(f32c(L0_attn_W_in).T)
    w_dtT = np.ascontiguousarray(f32c(L0_attn_W_dt).T)
    w_outT = np.ascontiguousarray(f32c(L0_attn_W_out).T)
    w_norm = f32c(L0_attn_w_norm).reshape(1, D)
    b_dt = f32c(L0_attn_b_dt).reshape(1, INNER)
    wup_nat = f32c(L0_ffn_w_up)
    wgate_nat = f32c(L0_ffn_w_gate)
    wdown_nat = f32c(L0_ffn_w_down)
    wupT = np.ascontiguousarray(wup_nat.T)
    wgateT = np.ascontiguousarray(wgate_nat.T)
    wdownT = np.ascontiguousarray(wdown_nat.T)
    ident = np.eye(128, dtype=np.float32)
    e32 = _make_e(32, 8)
    e96 = _make_e(96, 24)

    r1 = _run("L1", build_L1, [dict(x=xs[i], w_inT=w_inT, w_dtT=w_dtT,
                                    w_outT=w_outT, w_norm=w_norm, b_dt=b_dt,
                                    c_ident=ident) for i in range(NCORES)])
    m1all = np.concatenate([r1[i]["m1loc"].reshape(1, 1)
                            for i in range(NCORES)], axis=1)

    r2 = _run("L2", build_L2, [dict(x=xs[i], m1all=m1all, wup_nat=wup_nat,
                                    wgate_nat=wgate_nat, c_ident=ident)
                               for i in range(NCORES)])
    m23 = np.stack([r2[i]["m23loc"].reshape(2) for i in range(NCORES)], axis=1)
    m2all = np.ascontiguousarray(m23[0:1, :])
    m3all = np.ascontiguousarray(m23[1:2, :])

    r3 = _run("L3", build_L3, [dict(gate=r2[i]["gate"], m3all=m3all,
                                    c_ident=ident) for i in range(NCORES)])
    m4all = np.concatenate([r3[i]["m4loc"].reshape(1, 1)
                            for i in range(NCORES)], axis=1)

    r4 = _run("L4", build_L4, [dict(gate2=r3[i]["gate2"], up=r2[i]["up"],
                                    m2all=m2all, m4all=m4all, c_ident=ident)
                               for i in range(NCORES)])
    m5all = np.concatenate([r4[i]["m5loc"].reshape(1, 1)
                            for i in range(NCORES)], axis=1)

    r5 = _run("L5", build_L5, [dict(o_in=r4[i]["o_out"], m5all=m5all,
                                    wdown_nat=wdown_nat,
                                    partial=r1[i]["partial"], c_ident=ident)
                               for i in range(NCORES)])

    out = np.empty((16, SEQ, D), np.float32)
    for i in range(NCORES):
        out[2 * i:2 * i + 2] = r5[i]["outp"].reshape(2, SEQ, D)
    return out


# version 03
# speedup vs baseline: 1.0376x; 1.0376x over previous
from contextlib import ExitStack

import numpy as np

import concourse.bass as bass
import concourse.mybir as mybir
from concourse.bass_utils import run_bass_kernel_spmd

F32 = mybir.dt.float32
BF16 = mybir.dt.bfloat16
ALU = mybir.AluOpType
AF = mybir.ActivationFunctionType
AX = mybir.AxisListType.X
MAGIC = float(np.float32(1.5 * 2 ** 23))
NCORES = 8
SEQ = 2048
D = 1024
INNER = 2048
FF = 3072
NTOK = 2 * SEQ
QMAX = 32767.0
EPS = 1e-5


class Sem:
    __slots__ = ("h", "n")

    def __init__(self, h):
        self.h = h
        self.n = 0

    def inc(self, inst, k=1):
        inst.then_inc(self.h, k)
        self.n += k
        return (self, self.n)

    def tok(self):
        return (self, self.n)


_ENGINE_SEMS = ("v_self", "a_self", "p_self", "t_self")


class Ctx:
    def __init__(self, name, detect_races=True):
        self.nc = bass.Bass("TRN2", target_bir_lowering=False, name=name,
                            detect_race_conditions=detect_races)
        self.es = ExitStack()
        self.alloc_es = self.es
        self.tag = ""
        self._sems = {}

    def sem(self, name) -> Sem:
        if name not in _ENGINE_SEMS:
            name = self.tag + name
        if name not in self._sems:
            self._sems[name] = Sem(self.es.enter_context(self.nc.semaphore(name)))
        return self._sems[name]

    def sbuf(self, name, shape, dtype, **kw):
        return self.alloc_es.enter_context(
            self.nc.sbuf_tensor(self.tag + name, shape, dtype, **kw))

    def psum(self, name, shape, dtype=F32):
        return self.alloc_es.enter_context(
            self.nc.psum_tensor(self.tag + name, shape, dtype))

    def dram(self, name, shape, dtype=F32, kind="Internal", **kw):
        return self.nc.dram_tensor(name, shape, dtype, kind=kind, **kw).ap()

    def ex(self, eng, selfname, f, deps=()):
        s = self.sem(selfname)
        for d in deps:
            if d is not None:
                eng.wait_ge(d[0].h, d[1])
        if s.n:
            eng.wait_ge(s.h, s.n)
        return s.inc(f())

    def dma(self, eng, semname, out, in_, deps=(), **kw):
        s = self.sem(semname)
        for d in deps:
            if d is not None:
                eng.wait_ge(d[0].h, d[1])
        return s.inc(eng.dma_start(out=out, in_=in_, **kw), 16)


V = "v_self"
A = "a_self"
P = "p_self"
T = "t_self"


def emit_consts(ctx, block, values):
    cb = ctx.sbuf("cconsts", [128, max(len(values), 1)], F32)
    aps = {}
    tok = [None]

    @block.vector
    def _(v):
        for i, val in enumerate(values):
            tok[0] = ctx.ex(v, V, lambda i=i, val=val: v.memset(cb[:, i:i + 1],
                                                               float(val)))
            aps[val] = cb[:, i:i + 1]
    return aps, tok[0]



def emit_scalar_prep(ctx, block, m_in_sb, n_in, exprs, scb, ps, ones, dep_ld):
    nc = ctx.nc
    scal = ctx.sbuf("scal_" + m_in_sb.name, [1, 8], F32)

    toks = {}

    @block.vector
    def _(v):
        ctx.ex(v, V, lambda: v.memset(scal[:], 0.0))
        t = ctx.ex(v, V, lambda: v.reduce_max(scal[:, 1:2], m_in_sb[0:1, 0:n_in],
                                              axis=AX), deps=[dep_ld])
        t = ctx.ex(v, V, lambda: v.tensor_scalar(
            out=scal[:, 1:2], in0=scal[:, 1:2], scalar1=float(np.float32(1.0) / np.float32(QMAX)),
            scalar2=1e-12, op0=ALU.mult, op1=ALU.max))
        t = ctx.ex(v, V, lambda: v.reciprocal(scal[:, 0:1], scal[:, 1:2]))
        for (dst, src, s1, op) in exprs:
            t = ctx.ex(v, V, lambda dst=dst, src=src, s1=s1, op=op:
                       v.tensor_scalar(out=scal[:, dst:dst + 1],
                                       in0=scal[:, src:src + 1],
                                       scalar1=s1, scalar2=None, op0=op))
        toks["v"] = t

    @block.tensor
    def _(t):
        tk = ctx.ex(t, T, lambda: t.matmul(ps[:, 0:8], ones[0:1, :],
                                           scal[0:1, 0:8], start=True, stop=True),
                    deps=[toks["v"]])
        toks["t"] = tk

    @block.scalar
    def _(a):
        toks["a"] = ctx.ex(a, A, lambda: a.copy(scb[:, 0:8], ps[:, 0:8]),
                           deps=[toks["t"]])

    return toks["a"]


def emit_colmax_to_out(ctx, block, mxcol, ident, out_dram, n_out, dep_tok,
                       tag, ident_tok=None, ps_view=None, extra_dep=None):
    psx = (ps_view[0:max(n_out, 1), 0:128] if ps_view is not None
           else ctx.psum("psx_" + tag, [max(n_out, 1), 128], F32))
    scratch = ctx.__dict__.get("colmax_scratch")
    if scratch is not None:
        row = scratch[0:max(n_out, 1), 0:128]
        res = scratch[0:max(n_out, 1), 128:129]
    else:
        row = ctx.sbuf("mrow_" + tag, [max(n_out, 1), 128], F32)
        res = ctx.sbuf("mres_" + tag, [max(n_out, 1), 1], F32)
    toks = {}

    @block.tensor
    def _(t):
        toks["t"] = ctx.ex(t, T, lambda: t.transpose(psx[:, :], mxcol[:, 0:n_out],
                                                     ident[:]),
                           deps=[dep_tok, ident_tok, extra_dep])

    @block.vector
    def _(v):
        tk = ctx.ex(v, V, lambda: v.tensor_copy(row[:], psx[:, :]),
                    deps=[toks["t"]])
        toks["v"] = ctx.ex(v, V, lambda: v.reduce_max(res[:], row[:], axis=AX))

    @block.sync
    def _(sync):
        return_tok = ctx.dma(sync, "st_" + tag, out_dram, res[0:n_out, 0:1],
                             deps=[toks["v"]])
        toks["st"] = return_tok
    return toks["st"]


def emit_L3(ctx, block, gate, gate2, scb, scb_tok, mx, mxr, bufs, carry):
    ncols = gate.shape[1]
    n_rows = FF // 128
    gin, t_r, t_sig, gout = bufs["gin"], bufs["t_r"], bufs["t_sig"], bufs["gout"]
    ld_toks = carry.setdefault("ld", [None, None])
    r_toks = carry.setdefault("r", [None, None])
    sig_toks = carry.setdefault("sig", [None, None])
    g2_toks = carry.setdefault("g2", [None, None])
    st_toks = carry.setdefault("st", [None, None])
    mx_tok = carry.setdefault("mx", [None])

    for i in range(n_rows):
        p = i % 2

        @block.sync
        def _(sync, i=i, p=p):
            deps = [r_toks[p], st_toks[p]]
            ld_toks[p] = ctx.dma(sync, f"ldg{p}", gin[p][:, 0:ncols],
                                 gate[i * 128:(i + 1) * 128, :], deps=deps)

        @block.vector
        def _(v, i=i, p=p):
            ctx.ex(v, V, lambda: v.tensor_scalar(
                out=t_r[p][:, 0:ncols], in0=gin[p][:, 0:ncols],
                scalar1=scb[:, 0:1], scalar2=MAGIC, op0=ALU.mult, op1=ALU.add),
                deps=[ld_toks[p], scb_tok, sig_toks[p]])
            r_toks[p] = ctx.ex(v, V, lambda: v.tensor_scalar(
                out=t_r[p][:, 0:ncols], in0=t_r[p][:, 0:ncols], scalar1=MAGIC,
                scalar2=None, op0=ALU.subtract))

        @block.scalar
        def _(a, i=i, p=p):
            sig_toks[p] = ctx.ex(a, A, lambda: a.activation(
                out=t_sig[p][:, 0:ncols], in_=t_r[p][:, 0:ncols],
                func=AF.Sigmoid, scale=scb[:, 1:2]),
                deps=[r_toks[p], g2_toks[p]])

        @block.vector
        def _(v, i=i, p=p):
            ctx.ex(v, V, lambda: v.tensor_scalar(
                out=t_sig[p][:, 0:ncols], in0=t_sig[p][:, 0:ncols],
                scalar1=65536.0, scalar2=MAGIC, op0=ALU.mult, op1=ALU.add),
                deps=[sig_toks[p]])
            ctx.ex(v, V, lambda: v.tensor_scalar(
                out=t_sig[p][:, 0:ncols], in0=t_sig[p][:, 0:ncols],
                scalar1=MAGIC, scalar2=65535.0, op0=ALU.subtract, op1=ALU.min))
            g2_toks[p] = ctx.ex(v, V, lambda: v.scalar_tensor_tensor(
                out=gout[p][:, 0:ncols], in0=t_r[p][:, 0:ncols],
                scalar=scb[:, 2:3], in1=t_sig[p][:, 0:ncols],
                op0=ALU.mult, op1=ALU.mult),
                deps=[st_toks[p]])
            ctx.ex(v, V, lambda: v.tensor_reduce(
                out=mxr[:], in_=gout[p][:, 0:ncols], axis=AX, op=ALU.max,
                apply_absolute_value=True))
            mx_tok[0] = ctx.ex(v, V, lambda: v.tensor_max(mx[:], mx[:], mxr[:]))

        @block.scalar
        def _(a, i=i, p=p):
            st_toks[p] = ctx.dma(a, f"stg{p}",
                                 gate2[i * 128:(i + 1) * 128, :],
                                 gout[p][:, 0:ncols], deps=[g2_toks[p]])
    return mx_tok[0]


def build_L3():
    ctx = Ctx("L3")
    nc = ctx.nc
    gate = ctx.dram("gate", [FF, NTOK], kind="ExternalInput")
    m3all = ctx.dram("m3all", [1, 8], kind="ExternalInput")
    c_ident = ctx.dram("c_ident", [128, 128], kind="ExternalInput")
    gate2 = ctx.dram("gate2", [FF, NTOK], kind="ExternalOutput")
    m4loc = ctx.dram("m4loc", [1, 1], kind="ExternalOutput")

    with ctx.es:
        sm3 = ctx.sbuf("sm3", [1, 8], F32)
        ones = ctx.sbuf("ones", [1, 128], F32)
        ident = ctx.sbuf("ident", [128, 128], F32)
        scb = ctx.sbuf("scb", [128, 8], F32)
        bufs = dict(
            gin=[ctx.sbuf(f"gin{i}", [128, NTOK], F32) for i in range(2)],
            t_r=[ctx.sbuf(f"t_r{i}", [128, NTOK], F32) for i in range(2)],
            t_sig=[ctx.sbuf(f"t_sig{i}", [128, NTOK], F32) for i in range(2)],
            gout=[ctx.sbuf(f"gout{i}", [128, NTOK], F32) for i in range(2)])
        mx = ctx.sbuf("mx", [128, 1], F32)
        mxr = ctx.sbuf("mxr", [128, 1], F32)
        ps = ctx.psum("ps", [128, 8], F32)

        block = ctx.es.enter_context(nc.Block())

        @block.sync
        def _(sync):
            ctx.dma(sync, "ld0", sm3[:], m3all)
            ctx.dma(sync, "ld0", ident[:], c_ident)

        @block.vector
        def _(v):
            ctx.ex(v, V, lambda: v.memset(ones[:], 1.0))
            ctx.ex(v, V, lambda: v.memset(mx[:], 0.0))

        scb_tok = emit_scalar_prep(
            ctx, block, sm3, 8, [(2, 1, 1.0 / 65536.0, ALU.mult)],
            scb, ps, ones, ctx.sem("ld0").tok())

        mx_tok = emit_L3(ctx, block, gate, gate2, scb, scb_tok, mx, mxr,
                         bufs, {})

        emit_colmax_to_out(ctx, block, mx, ident, m4loc, 1, mx_tok, "m4",
                           ident_tok=ctx.sem("ld0").tok())

        @block.sync
        def _(sync):
            sync.wait_ge(ctx.sem("stg0").h, ctx.sem("stg0").n)
            sync.wait_ge(ctx.sem("stg1").h, ctx.sem("stg1").n)
            sync.wait_ge(ctx.sem("st_m4").h, ctx.sem("st_m4").n)

    return ctx.nc


def emit_L4(ctx, block, g2, up, o_out, scb2, scb4, prep_toks, mx, mxr, bufs,
            carry):
    ncols = g2.shape[1]
    n_rows = FF // 128
    g2in, upin, t_g, t_u, oout = (bufs["g2in"], bufs["upin"], bufs["t_g"],
                                  bufs["t_u"], bufs["oout"])
    cMAG = bufs["cMAG"]
    cNMAG = bufs["cNMAG"]
    ldg = carry.setdefault("ldg", [None, None])
    ldu = carry.setdefault("ldu", [None, None])
    o_toks = carry.setdefault("o", [None, None])
    uq_toks = carry.setdefault("uq", [None, None])
    gq_toks = carry.setdefault("gq", [None, None])
    st_toks = carry.setdefault("st", [None, None])
    mx_tok = carry.setdefault("mx", [None])

    for i in range(n_rows):
        p = i % 2

        @block.sync
        def _(sync, i=i, p=p):
            ldg[p] = ctx.dma(sync, f"ldg{p}", g2in[p][:, 0:ncols],
                             g2[i * 128:(i + 1) * 128, :], deps=[gq_toks[p]])

        @block.scalar
        def _(a, i=i, p=p):
            ldu[p] = ctx.dma(a, f"ldu{p}", upin[p][:, 0:ncols],
                             up[i * 128:(i + 1) * 128, :], deps=[uq_toks[p]])
            ctx.ex(a, A, lambda: a.activation(
                out=t_u[p][:, 0:ncols], in_=upin[p][:, 0:ncols],
                func=AF.Identity, scale=scb2[:, 0:1], bias=cMAG),
                deps=[ldu[p], o_toks[p]] + (list(prep_toks) if i < 2 else []))
            ctx.ex(a, A, lambda: a.activation(
                out=t_u[p][:, 0:ncols], in_=t_u[p][:, 0:ncols],
                func=AF.Identity, bias=cNMAG))
            uq_toks[p] = ctx.ex(a, A, lambda: a.activation(
                out=t_u[p][:, 0:ncols], in_=t_u[p][:, 0:ncols],
                func=AF.Identity, scale=scb2[:, 1:2]))

        @block.vector
        def _(v, i=i, p=p):
            ctx.ex(v, V, lambda: v.tensor_scalar(
                out=t_g[p][:, 0:ncols], in0=g2in[p][:, 0:ncols],
                scalar1=scb4[:, 0:1], scalar2=MAGIC, op0=ALU.mult,
                op1=ALU.add),
                deps=[ldg[p]] + (list(prep_toks) if i < 2 else []))
            gq_toks[p] = ctx.ex(v, V, lambda: v.tensor_scalar(
                out=t_g[p][:, 0:ncols], in0=t_g[p][:, 0:ncols], scalar1=MAGIC,
                scalar2=None, op0=ALU.subtract))
            o_toks[p] = ctx.ex(v, V, lambda: v.scalar_tensor_tensor(
                out=oout[p][:, 0:ncols], in0=t_g[p][:, 0:ncols],
                scalar=scb4[:, 1:2], in1=t_u[p][:, 0:ncols],
                op0=ALU.mult, op1=ALU.mult),
                deps=[uq_toks[p], st_toks[p]])
            ctx.ex(v, V, lambda: v.tensor_reduce(
                out=mxr[:], in_=oout[p][:, 0:ncols], axis=AX, op=ALU.max,
                apply_absolute_value=True))
            mx_tok[0] = ctx.ex(v, V, lambda: v.tensor_max(mx[:], mx[:], mxr[:]))

        @block.sync
        def _(sync, i=i, p=p):
            st_toks[p] = ctx.dma(sync, f"stg{p}",
                                 o_out[i * 128:(i + 1) * 128, :],
                                 oout[p][:, 0:ncols], deps=[o_toks[p]])
    return mx_tok[0]


def build_L4():
    ctx = Ctx("L4")
    nc = ctx.nc
    g2 = ctx.dram("gate2", [FF, NTOK], kind="ExternalInput")
    up = ctx.dram("up", [FF, NTOK], kind="ExternalInput")
    m2all = ctx.dram("m2all", [1, 8], kind="ExternalInput")
    m4all = ctx.dram("m4all", [1, 8], kind="ExternalInput")
    c_ident = ctx.dram("c_ident", [128, 128], kind="ExternalInput")
    o_out = ctx.dram("o_out", [FF, NTOK], kind="ExternalOutput")
    m5loc = ctx.dram("m5loc", [1, 1], kind="ExternalOutput")

    with ctx.es:
        sm2 = ctx.sbuf("sm2", [1, 8], F32)
        sm4 = ctx.sbuf("sm4", [1, 8], F32)
        ones = ctx.sbuf("ones", [1, 128], F32)
        ident = ctx.sbuf("ident", [128, 128], F32)
        scb2 = ctx.sbuf("scb2", [128, 8], F32)
        scb4 = ctx.sbuf("scb4", [128, 8], F32)
        bufs = dict(
            g2in=[ctx.sbuf(f"g2in{i}", [128, NTOK], F32) for i in range(2)],
            upin=[ctx.sbuf(f"upin{i}", [128, NTOK], F32) for i in range(2)],
            t_g=[ctx.sbuf(f"t_g{i}", [128, NTOK], F32) for i in range(2)],
            t_u=[ctx.sbuf(f"t_u{i}", [128, NTOK], F32) for i in range(2)],
            oout=[ctx.sbuf(f"oout{i}", [128, NTOK], F32) for i in range(2)])
        mx = ctx.sbuf("mx", [128, 1], F32)
        mxr = ctx.sbuf("mxr", [128, 1], F32)
        ps = ctx.psum("ps", [128, 16], F32)

        block = ctx.es.enter_context(nc.Block())

        @block.sync
        def _(sync):
            ctx.dma(sync, "ld0", sm2[:], m2all)
            ctx.dma(sync, "ld0", sm4[:], m4all)
            ctx.dma(sync, "ld0", ident[:], c_ident)

        @block.vector
        def _(v):
            ctx.ex(v, V, lambda: v.memset(ones[:], 1.0))
            ctx.ex(v, V, lambda: v.memset(mx[:], 0.0))

        capd, cst_tok = emit_consts(ctx, block, [MAGIC, -MAGIC])
        bufs["cMAG"], bufs["cNMAG"] = capd[MAGIC], capd[-MAGIC]
        t2 = emit_scalar_prep(ctx, block, sm2, 8, [], scb2, ps[0:128, 0:8],
                              ones, ctx.sem("ld0").tok())
        t4 = emit_scalar_prep(ctx, block, sm4, 8, [], scb4, ps[0:128, 8:16],
                              ones, ctx.sem("ld0").tok())

        mx_tok = emit_L4(ctx, block, g2, up, o_out, scb2, scb4,
                         [t2, t4, cst_tok], mx, mxr, bufs, {})

        emit_colmax_to_out(ctx, block, mx, ident, m5loc, 1, mx_tok, "m5",
                           ident_tok=ctx.sem("ld0").tok())

        @block.sync
        def _(sync):
            sync.wait_ge(ctx.sem("stg0").h, ctx.sem("stg0").n)
            sync.wait_ge(ctx.sem("stg1").h, ctx.sem("stg1").n)
            sync.wait_ge(ctx.sem("st_m5").h, ctx.sem("st_m5").n)

    return ctx.nc


def emit_ln_tile(ctx, block, xt, xnb, st6, mv, sd, dep_ld, extra_deps=()):
    toks = {}

    @block.vector
    def _(v):
        ctx.ex(v, V, lambda: v.bn_stats(st6[:, 0:6], xt[:, 0:512]),
               deps=[dep_ld, *extra_deps])
        ctx.ex(v, V, lambda: v.bn_stats(st6[:, 6:12], xt[:, 512:1024]))
        toks["mv"] = ctx.ex(v, V, lambda: v.bn_aggr(mv[:], st6[:]))

    @block.scalar
    def _(a):
        toks["sd"] = ctx.ex(a, A, lambda: a.activation(
            out=sd[:, 0:1], in_=mv[:, 1:2], func=AF.Sqrt, bias=EPS),
            deps=[toks["mv"]])

    @block.vector
    def _(v):
        ctx.ex(v, V, lambda: v.reciprocal(sd[:, 1:2], sd[:, 0:1]),
               deps=[toks["sd"]])
        toks["xn"] = ctx.ex(v, V, lambda: v.tensor_scalar(
            out=xnb[:], in0=xt[:], scalar1=mv[:, 0:1], scalar2=sd[:, 1:2],
            op0=ALU.subtract, op1=ALU.mult))
    return toks["xn"]


def emit_ln_tile(ctx, block, xt, xnb, st6, mv, sd, dep_ld, carry, eps_ap,
                 extra_deps=()):
    toks = {}

    @block.vector
    def _(v):
        ctx.ex(v, V, lambda: v.bn_stats(st6[:, 0:6], xt[:, 0:512]),
               deps=[dep_ld, *extra_deps])
        ctx.ex(v, V, lambda: v.bn_stats(st6[:, 6:12], xt[:, 512:1024]))
        toks["mv"] = ctx.ex(v, V, lambda: v.bn_aggr(mv[:], st6[:]),
                            deps=[carry.get("sd")])

    @block.scalar
    def _(a):
        toks["sd"] = ctx.ex(a, A, lambda: a.activation(
            out=sd[:, 0:1], in_=mv[:, 1:2], func=AF.Sqrt, bias=eps_ap),
            deps=[toks["mv"], carry.get("recip"), carry.get("cst")])

    @block.vector
    def _(v):
        carry["recip"] = ctx.ex(v, V, lambda: v.reciprocal(sd[:, 1:2], sd[:, 0:1]),
                                deps=[toks["sd"]])
        toks["xn"] = ctx.ex(v, V, lambda: v.tensor_scalar(
            out=xnb[:], in0=xt[:], scalar1=mv[:, 0:1], scalar2=sd[:, 1:2],
            op0=ALU.subtract, op1=ALU.mult), deps=[carry.get("xnb_rd")])
    carry["sd"] = toks["sd"]
    return toks["xn"]


def emit_weight_dequant(ctx, block, tag, w_nat, O, I, wq_dst, bufs,
                        entry_deps=(), carry=None):
    n_nat = O // 128
    nblk = I // 32
    natb, amb, wqn = bufs["natb"], bufs["amb"], bufs["wqn"]
    carry = carry if carry is not None else {}
    nat_rd = carry.get("nat_rd", [None, None])
    nat_ld = [None, None]
    xp_tok = carry.get("xp_tok", [None, None])
    for t in range(n_nat):
        p = t % 2

        @block.scalar
        def _(a, t=t, p=p):
            deps = [nat_rd[p]] + (list(entry_deps) if t < 2 else [])
            nat_ld[p] = ctx.dma(a, f"{tag}nat{p}", natb[p][:, 0:I],
                                w_nat[t * 128:(t + 1) * 128, :], deps=deps)

        @block.vector
        def _(v, t=t, p=p):
            am = amb[:, p * nblk:(p + 1) * nblk]
            ctx.ex(v, V, lambda: v.reduce_max(
                am, natb[p][:, 0:I].rearrange("p (b e) -> p b e", e=32),
                axis=AX, apply_absolute_value=True), deps=[nat_ld[p]])
            ctx.ex(v, V, lambda: v.tensor_scalar(
                out=am, in0=am,
                scalar1=float(np.float32(1.0) / np.float32(7.0)),
                scalar2=1e-12, op0=ALU.mult, op1=ALU.max))
            ctx.ex(v, V, lambda: v.reciprocal(amb[:, 2 * nblk:2 * nblk + nblk],
                                              am))
            ctx.ex(v, V, lambda: v.tensor_tensor(
                out=natb[p][:, 0:I].rearrange("p (b e) -> p b e", e=32),
                in0=natb[p][:, 0:I].rearrange("p (b e) -> p b e", e=32),
                in1=amb[:, 2 * nblk:2 * nblk + nblk].unsqueeze(2)
                    .broadcast_to([128, nblk, 32]),
                op=ALU.mult))
            ctx.ex(v, V, lambda: v.tensor_scalar(
                out=natb[p][:, 0:I], in0=natb[p][:, 0:I], scalar1=MAGIC,
                scalar2=MAGIC, op0=ALU.add, op1=ALU.subtract),
                deps=[xp_tok[p]])
            nat_rd[p] = ctx.ex(v, V, lambda: v.tensor_tensor(
                out=wqn[p][:, 0:I].rearrange("p (b e) -> p b e", e=32),
                in0=natb[p][:, 0:I].rearrange("p (b e) -> p b e", e=32),
                in1=am.unsqueeze(2).broadcast_to([128, nblk, 32]),
                op=ALU.mult))

        @block.sync
        def _(sync, t=t, p=p):
            outv = wq_dst[:].rearrange("p (k o) -> p k o", o=O)[
                :, :, t * 128:(t + 1) * 128]
            inv = wqn[p][:, 0:I].rearrange("p (k c) -> p k c", c=128)
            xp_tok[p] = ctx.dma(sync, f"{tag}xp{p}", outv, inv,
                                deps=[nat_rd[p]], transpose=True)
    carry["nat_rd"], carry["xp_tok"] = nat_rd, xp_tok
    return xp_tok[(n_nat - 1) % 2], xp_tok[n_nat % 2]


def emit_mm_fm(ctx, block, tag, wq, rhs_fm, n_k, out_dram, n_m, mxcol, mxr,
               outb, ps_mm, wq_tok, rhs_toks, O, carry=None):
    n_tt = NTOK // 512
    carry = carry or {}
    evac = carry.get("evac", [None, None])
    st = carry.get("st", [None, None])
    red = carry.get("red", [None, None])
    mx_tok = [None]
    first_mx = [True]
    gi = 0
    for m in range(n_m):
        for tt in range(n_tt):
            p = gi % 2

            @block.tensor
            def _(t_, m=m, tt=tt, p=p, gi=gi):
                deps = ([wq_tok] + list(rhs_toks) if gi == 0 else []) + [evac[p]]
                for d in deps:
                    if d is not None:
                        t_.wait_ge(d[0].h, d[1])
                last = None
                for k in range(n_k):
                    last = t_.matmul(
                        ps_mm[p][:, 0:512],
                        wq[:, k * O + m * 128:k * O + (m + 1) * 128],
                        rhs_fm[:, k * NTOK + tt * 512:k * NTOK + (tt + 1) * 512],
                        start=(k == 0), stop=(k == n_k - 1))
                ctx.sem(f"{tag}mm").inc(last)

            @block.vector
            def _(v, m=m, tt=tt, p=p):
                ctx.ex(v, V, lambda: v.tensor_copy(outb[p][:], ps_mm[p][:, 0:512]),
                       deps=[ctx.sem(f"{tag}mm").tok(), st[p], red[p]])
                evac[p] = ctx.sem(V).tok()

            @block.vector
            def _(v, m=m, tt=tt, p=p):
                red[p] = ctx.ex(v, V, lambda: v.tensor_reduce(
                    out=mxr[:], in_=outb[p][:], axis=AX, op=ALU.max,
                    apply_absolute_value=True), deps=[evac[p]])
                if first_mx[0]:
                    mx_tok[0] = ctx.ex(v, V, lambda: v.tensor_copy(mxcol[:], mxr[:]))
                    first_mx[0] = False
                else:
                    mx_tok[0] = ctx.ex(v, V, lambda: v.tensor_max(
                        mxcol[:], mxcol[:], mxr[:]))

            @block.sync
            def _(sync, m=m, tt=tt, p=p):
                st[p] = ctx.dma(
                    sync, f"{tag}st{p}",
                    out_dram[m * 128:(m + 1) * 128, tt * 512:(tt + 1) * 512],
                    outb[p][:], deps=[evac[p]])
            gi += 1
    carry["evac"], carry["st"], carry["red"] = evac, st, red
    return st, mx_tok[0], carry


def build_L2():
    ctx = Ctx("L2")
    nc = ctx.nc
    x = ctx.dram("x", [NTOK, D], kind="ExternalInput")
    m1all = ctx.dram("m1all", [1, 8], kind="ExternalInput")
    wup_nat = ctx.dram("wup_nat", [FF, D], kind="ExternalInput")
    wgate_nat = ctx.dram("wgate_nat", [FF, D], kind="ExternalInput")
    c_ident = ctx.dram("c_ident", [128, 128], kind="ExternalInput")
    up_o = ctx.dram("up", [FF, NTOK], kind="ExternalOutput")
    gate_o = ctx.dram("gate", [FF, NTOK], kind="ExternalOutput")
    m23loc = ctx.dram("m23loc", [1, 2], kind="ExternalOutput")

    n_ttiles = NTOK // 128

    with ctx.es:
        sm1 = ctx.sbuf("sm1", [1, 8], F32)
        ones = ctx.sbuf("ones", [1, 128], F32)
        ident = ctx.sbuf("ident", [128, 128], F32)
        scb1 = ctx.sbuf("scb1", [128, 8], F32)
        st6 = ctx.sbuf("st6", [128, 12], F32)
        mv = ctx.sbuf("mv", [128, 2], F32)
        sd = ctx.sbuf("sd", [128, 2], F32)
        xqb = [ctx.sbuf(f"xqb{i}", [128, D], BF16) for i in range(2)]
        xq_fm = ctx.sbuf("xq_fm", [128, 8 * NTOK], BF16, is_transpose_dest=True)
        natb = [ctx.sbuf(f"natb{i}", [128, D], F32) for i in range(2)]
        xin = [ctx.sbuf(f"xin{i}", [128, D], F32) for i in range(2)]
        xnb = ctx.sbuf("xnb", [128, D], F32)
        amb = ctx.sbuf("amb", [128, 3 * 32], F32)
        wqn = [ctx.sbuf(f"wqn{i}", [128, D], BF16, is_transpose_dest=False)
               for i in range(2)]
        wq2 = [ctx.sbuf(f"wq2_{i}", [128, 8 * FF], BF16, is_transpose_dest=True)
               for i in range(2)]
        outb = [ctx.sbuf(f"outb{i}", [128, 512], F32) for i in range(2)]
        mxr = ctx.sbuf("mxr", [128, 1], F32)
        mx23 = ctx.sbuf("mx23", [128, 2], F32)
        ps_sc = ctx.psum("ps_sc", [128, 128], F32)
        ps_mm0 = ctx.psum("ps_mm0", [128, 512], F32)
        ps_mm1 = ctx.psum("ps_mm1", [128, 512], F32)

        block = ctx.es.enter_context(nc.Block())

        @block.sync
        def _(sync):
            ctx.dma(sync, "ld0", sm1[:], m1all)
            ctx.dma(sync, "ld0", ident[:], c_ident)
        ld0 = ctx.sem("ld0").tok()

        @block.vector
        def _(v):
            ctx.ex(v, V, lambda: v.memset(ones[:], 1.0))

        capd, cst_tok = emit_consts(ctx, block, [EPS])
        scb_tok = emit_scalar_prep(ctx, block, sm1, 8, [], scb1,
                                   ps_sc[0:128, 0:8], ones, ld0)

        x_ld = [None, None]
        xq_rd = [None, None]
        xp_sem = [ctx.sem("xp0"), ctx.sem("xp1")]
        for t in range(n_ttiles):
            p = t % 2

            @block.sync
            def _(sync, t=t, p=p):
                x_ld[p] = ctx.dma(sync, f"xld{p}", xin[p][:],
                                  x[t * 128:(t + 1) * 128, :],
                                  deps=[xq_rd[p]])

            if t == 0:
                ln_carry = {"cst": cst_tok}
            xn_tok = emit_ln_tile(ctx, block, xin[p], xnb, st6, mv, sd,
                                  x_ld[p], ln_carry, capd[EPS])

            @block.vector
            def _(v, t=t, p=p):
                ctx.ex(v, V, lambda: v.tensor_scalar(
                    out=xnb[:], in0=xnb[:], scalar1=scb1[:, 0:1], scalar2=MAGIC,
                    op0=ALU.mult, op1=ALU.add),
                    deps=[scb_tok, xp_sem[p].tok()])
                xq_rd[p] = ctx.ex(v, V, lambda: v.tensor_scalar(
                    out=xqb[p][:], in0=xnb[:], scalar1=MAGIC, scalar2=scb1[:, 1:2],
                    op0=ALU.subtract, op1=ALU.mult))

            @block.scalar
            def _(a, t=t, p=p):
                outv = xq_fm[:].rearrange("p (k q) -> p k q", q=NTOK)[
                    :, :, t * 128:(t + 1) * 128]
                inv = xqb[p][:].rearrange("p (k c) -> p k c", c=128)
                ctx.dma(a, f"xp{p}", outv, inv, deps=[xq_rd[p]],
                        transpose=True)
        xq_all = (ctx.sem("xp0"), ctx.sem("xp0").n)
        xq_all1 = (ctx.sem("xp1"), ctx.sem("xp1").n)

        bufs = dict(natb=natb, amb=amb, wqn=wqn)
        dq_carry = {}

        uq1, uq2 = emit_weight_dequant(ctx, block, "u", wup_nat, FF, D,
                                       wq2[0], bufs, carry=dq_carry)
        st_u, mx_u, mm_carry = emit_mm_fm(
            ctx, block, "u", wq2[0], xq_fm, D // 128, up_o, FF // 128,
            mx23[:, 0:1], mxr, outb, [ps_mm0, ps_mm1], uq1,
            [uq2, xq_all, xq_all1], FF)

        gq1, gq2 = emit_weight_dequant(ctx, block, "g", wgate_nat, FF, D,
                                       wq2[1], bufs, carry=dq_carry)
        st_g, mx_g, _ = emit_mm_fm(
            ctx, block, "g", wq2[1], xq_fm, D // 128, gate_o, FF // 128,
            mx23[:, 1:2], mxr, outb, [ps_mm0, ps_mm1], gq1, [gq2],
            FF, carry=mm_carry)

        emit_colmax_to_out(ctx, block, mx23, ident, m23loc, 2, mx_g, "m23",
                           ident_tok=ld0)

        @block.sync
        def _(sync):
            for s in ("ust0", "ust1", "gst0", "gst1", "st_m23"):
                sync.wait_ge(ctx.sem(s).h, ctx.sem(s).n)

    return ctx.nc


def build_L5():
    ctx = Ctx("L5")
    nc = ctx.nc
    o_in = ctx.dram("o_in", [FF, NTOK], kind="ExternalInput")
    m5all = ctx.dram("m5all", [1, 8], kind="ExternalInput")
    wdown_nat = ctx.dram("wdown_nat", [D, FF], kind="ExternalInput")
    partial = ctx.dram("partial", [NTOK, D], kind="ExternalInput")
    c_ident = ctx.dram("c_ident", [128, 128], kind="ExternalInput")
    outp = ctx.dram("outp", [NTOK, D], kind="ExternalOutput")

    QT = 1024
    n_q = NTOK // QT
    n_k = FF // 128

    with ctx.es:
        sm5 = ctx.sbuf("sm5", [1, 8], F32)
        ones = ctx.sbuf("ones", [1, 128], F32)
        ident = ctx.sbuf("ident", [128, 128], F32)
        scb5 = ctx.sbuf("scb5", [128, 8], F32)
        natb = [ctx.sbuf(f"natb{i}", [128, FF], F32) for i in range(2)]
        amb = ctx.sbuf("amb", [128, 3 * 96], F32)
        wqn = [ctx.sbuf(f"wqn{i}", [128, FF], BF16) for i in range(2)]
        wq = ctx.sbuf("wq", [128, n_k * D], BF16, is_transpose_dest=True)
        oq = ctx.sbuf("oq", [128, n_k * QT], BF16)
        oin = [ctx.sbuf(f"oin{i}", [128, QT], F32) for i in range(2)]
        padd = [ctx.sbuf(f"padd{i}", [128, D], F32) for i in range(2)]
        outb = [ctx.sbuf(f"outb{i}", [128, D], F32) for i in range(2)]
        ps_all = ctx.psum("ps_all", [128, 4096], F32)

        block = ctx.es.enter_context(nc.Block())

        @block.sync
        def _(sync):
            ctx.dma(sync, "ld0", sm5[:], m5all)
            ctx.dma(sync, "ld0", ident[:], c_ident)
        ld0 = ctx.sem("ld0").tok()

        @block.vector
        def _(v):
            ctx.ex(v, V, lambda: v.memset(ones[:], 1.0))

        scb_tok = emit_scalar_prep(ctx, block, sm5, 8, [], scb5,
                                   ps_all[0:128, 0:8], ones, ld0)

        bufs = dict(natb=natb, amb=amb, wqn=wqn)
        wq_tok, wq_tok2 = emit_weight_dequant(ctx, block, "d", wdown_nat,
                                              D, FF, wq, bufs)

        o_ld = [None, None]
        o_rd = [None, None]
        oq_toks = [None] * n_k
        mm_done = [None]
        evac_done = [None] * 4
        pa_ld = [None, None]
        pa_rd = [None, None]
        st = [None, None]

        for q in range(n_q):
            for k in range(n_k):
                p = k % 2

                @block.sync
                def _(sync, q=q, k=k, p=p):
                    deps = [o_rd[p], mm_done[0] if k < 2 else None]
                    o_ld[p] = ctx.dma(sync, f"old{p}", oin[p][:],
                                      o_in[k * 128:(k + 1) * 128,
                                           q * QT:(q + 1) * QT], deps=deps)

                @block.vector
                def _(v, q=q, k=k, p=p):
                    ctx.ex(v, V, lambda: v.tensor_scalar(
                        out=oin[p][:], in0=oin[p][:], scalar1=scb5[:, 0:1],
                        scalar2=MAGIC, op0=ALU.mult, op1=ALU.add),
                        deps=[o_ld[p], scb_tok if (q == 0 and k < 2) else None])
                    o_rd[p] = oq_toks[k] = ctx.ex(v, V, lambda: v.tensor_scalar(
                        out=oq[:, k * QT:(k + 1) * QT], in0=oin[p][:],
                        scalar1=MAGIC, scalar2=scb5[:, 1:2],
                        op0=ALU.subtract, op1=ALU.mult))

            for g in range(4):
                gh = g % 2

                @block.tensor
                def _(t_, q=q, g=g, gh=gh):
                    deps = ([oq_toks[n_k - 1],
                             wq_tok if q == 0 and g == 0 else None,
                             wq_tok2 if q == 0 and g == 0 else None]
                            ) + [evac_done[gh * 2], evac_done[gh * 2 + 1]]
                    for d in deps:
                        if d is not None:
                            t_.wait_ge(d[0].h, d[1])
                    last = None
                    for k in range(n_k):
                        for c in range(2):
                            lhs = oq[:, k * QT + g * 256 + c * 128:
                                     k * QT + g * 256 + (c + 1) * 128]
                            for nn in range(2):
                                last = t_.matmul(
                                    ps_all[:, gh * 2048 + c * 1024 + nn * 512:
                                           gh * 2048 + c * 1024 + (nn + 1) * 512],
                                    lhs,
                                    wq[:, k * D + nn * 512:k * D + (nn + 1) * 512],
                                    start=(k == 0), stop=(k == n_k - 1))
                    mm_done[0] = ctx.sem("fmm").inc(last)

                for c in range(2):
                    pc = c % 2
                    r0 = q * QT + g * 256 + c * 128

                    @block.scalar
                    def _(a, r0=r0, pc=pc, c=c):
                        pa_ld[pc] = ctx.dma(a, f"pld{pc}", padd[pc][:],
                                            partial[r0:r0 + 128, :],
                                            deps=[pa_rd[pc]])

                    @block.vector
                    def _(v, r0=r0, pc=pc, c=c, gh=gh):
                        pa_rd[pc] = evac_done[gh * 2 + c] = ctx.ex(
                            v, V, lambda: v.tensor_tensor(
                                out=outb[pc][:], in0=padd[pc][:],
                                in1=ps_all[:, gh * 2048 + c * 1024:
                                           gh * 2048 + (c + 1) * 1024],
                                op=ALU.add),
                            deps=[pa_ld[pc], mm_done[0], st[pc]])

                    @block.sync
                    def _(sync, r0=r0, pc=pc, c=c, gh=gh):
                        st[pc] = ctx.dma(sync, f"ost{pc}", outp[r0:r0 + 128, :],
                                         outb[pc][:],
                                         deps=[evac_done[gh * 2 + c]])

        @block.sync
        def _(sync):
            for s in ("ost0", "ost1"):
                sync.wait_ge(ctx.sem(s).h, ctx.sem(s).n)

    return ctx.nc


def build_L1():
    ctx = Ctx("L1")
    nc = ctx.nc
    x_in = ctx.dram("x", [NTOK, D], kind="ExternalInput")
    w_inT = ctx.dram("w_inT", [D, 2 * INNER], kind="ExternalInput")
    w_dtT = ctx.dram("w_dtT", [INNER, INNER], kind="ExternalInput")
    w_outT = ctx.dram("w_outT", [INNER, D], kind="ExternalInput")
    w_norm = ctx.dram("w_norm", [1, D], kind="ExternalInput")
    b_dt = ctx.dram("b_dt", [1, INNER], kind="ExternalInput")
    c_ident = ctx.dram("c_ident", [128, 128], kind="ExternalInput")
    partial = ctx.dram("partial", [NTOK, D], kind="ExternalOutput")
    m1loc = ctx.dram("m1loc", [1, 1], kind="ExternalOutput")
    s2_dram = ctx.dram("s2_dram", [2, INNER, SEQ], BF16, kind="Internal")

    NT = SEQ // 128
    NKD = D // 128
    NJ = INNER // 128
    with ctx.es:
        ident = ctx.sbuf("ident", [128, 128], F32)
        ones = ctx.sbuf("ones", [1, 128], F32)
        wnb = ctx.sbuf("wnb", [128, D], F32)
        bdt = ctx.sbuf("bdt", [128, 16], F32)
        st6 = ctx.sbuf("st6", [128, 12], F32)
        mv = [ctx.sbuf(f"mv{i}", [128, 2], F32) for i in range(2)]
        rs2 = [ctx.sbuf(f"rs2{i}", [128, 4], F32) for i in range(2)]
        mx = ctx.sbuf("mx", [128, 1], F32)
        mxr = ctx.sbuf("mxr", [128, 1], F32)
        hbf = [ctx.sbuf(f"hbf{i}", [128, D], BF16) for i in range(2)]
        win = ctx.sbuf("win", [128, NKD * INNER], BF16)
        us = ctx.sbuf("us", [128, NJ * SEQ], BF16)
        h_fm = ctx.sbuf("h_fm", [128, NKD * SEQ], BF16, is_transpose_dest=True)
        wdtb = [ctx.sbuf(f"wdtb{i}", [128, NJ * 128], BF16) for i in range(2)]
        lam = ctx.sbuf("lam", [128, SEQ], F32)
        siluz = ctx.sbuf("siluz", [128, SEQ], F32)
        bneg = ctx.sbuf("bneg", [128, SEQ], F32)
        sneg = ctx.sbuf("sneg", [128, SEQ], F32)
        s2b = [ctx.sbuf(f"s2b{i}", [128, SEQ], BF16) for i in range(2)]
        s2k = [ctx.sbuf(f"s2k{i}", [128, 512], BF16) for i in range(2)]
        ps_all = ctx.psum("ps_all", [128, 4096], F32)

        xbuf = [ctx.sbuf(f"xbuf{i}", [128, D], F32) for i in range(2)]
        wn_row = xbuf[0][0:1, 0:D]
        bdt_raw = xbuf[1][0:16, 0:128]
        xnb = ctx.sbuf("xnb", [128, D], F32)
        sq = [ctx.sbuf(f"sqb{i}", [128, D], F32) for i in range(2)]
        padd = [bneg[:, 0:D], bneg[:, D:2 * D]]
        outb = [sneg[:, 0:D], sneg[:, D:2 * D]]
        wob = us[:, 0:NJ * D]

        block = ctx.es.enter_context(nc.Block())

        @block.sync
        def _(sync):
            ctx.dma(sync, "ld0", ident[:], c_ident)
            ctx.dma(sync, "ld0", wn_row[:], w_norm)
            ctx.dma(sync, "ld0", bdt_raw[:],
                    b_dt.rearrange("o (j p) -> (o j) p", p=128))
        ld0 = ctx.sem("ld0").tok()

        capd, cst_tok = emit_consts(ctx, block, [EPS])

        @block.vector
        def _(v):
            ctx.ex(v, V, lambda: v.memset(ones[:], 1.0))
            ctx.ex(v, V, lambda: v.memset(mx[:], 0.0))

        @block.tensor
        def _(t):
            ctx.ex(t, T, lambda: t.matmul(ps_all[:, 0:512], ones[0:1, :],
                                          wn_row[0:1, 0:512], start=True,
                                          stop=True),
                   deps=[ld0, ctx.sem(V).tok()])
            ctx.ex(t, T, lambda: t.matmul(ps_all[:, 512:1024], ones[0:1, :],
                                          wn_row[0:1, 512:1024], start=True,
                                          stop=True))
            ctx.ex(t, T, lambda: t.transpose(ps_all[0:128, 1024:1040],
                                             bdt_raw[0:16, :], ident[0:16, 0:16]))

        @block.scalar
        def _(a):
            ctx.ex(a, A, lambda: a.copy(wnb[:], ps_all[:, 0:1024]),
                   deps=[ctx.sem(T).tok()])
            ctx.ex(a, A, lambda: a.copy(bdt[:], ps_all[0:128, 1024:1040]))
        prep_tok = ctx.sem(A).tok()

        pstore = [[None, None], [None, None]]
        pstore_all = [[None] * NT, [None] * NT]
        hxp_all_toks = [[None] * NT, [None] * NT]
        mx_tok = [None]
        cmm_z_done = [None]
        lam_mm_done = [None]
        fmm_done = [None]
        z_done = [None]
        scan_tok = [None]
        s2_tok = [None]
        win_rd = [None]
        s2_ld = [None, None]
        s2_rd = [None, None]
        ev_done = [None] * 4
        pa_ld = [None, None]
        pa_rd = [None, None]
        ost = [None, None]
        ln_carry = {"cst": cst_tok}

        for s in range(2):
            x_ld = [None, None]
            x_rd = [None, None]
            hxp = [None, None]
            stats_tok = [None, None]
            sqrt_tok = [None, None]
            fin_tok = [None, None]

            def emit_stats(t):
                p = t % 2
                row0 = s * SEQ + t * 128

                @block.sync
                def _(sync, t=t, p=p, row0=row0):
                    deps = [x_rd[p], scan_tok[0] if t < 2 else None,
                            prep_tok if (s == 0 and t < 2) else None]
                    x_ld[p] = ctx.dma(sync, f"xld{p}", xbuf[p][:],
                                      x_in[row0:row0 + 128, :], deps=deps)

                @block.vector
                def _(v, t=t, p=p):
                    ctx.ex(v, V, lambda: v.bn_stats(st6[:, 0:6],
                                                    xbuf[p][:, 0:512]),
                           deps=[x_ld[p]])
                    ctx.ex(v, V, lambda: v.bn_stats(st6[:, 6:12],
                                                    xbuf[p][:, 512:1024]))
                    ctx.ex(v, V, lambda: v.bn_aggr(mv[p][:], st6[:]),
                           deps=[sqrt_tok[p]])
                    ctx.ex(v, V, lambda: v.tensor_scalar(
                        out=rs2[p][:, 1:2], in0=mv[p][:, 1:2], scalar1=1.0,
                        scalar2=EPS, op0=ALU.mult, op1=ALU.add))
                    ctx.ex(v, V, lambda: v.reciprocal(rs2[p][:, 0:1],
                                                      rs2[p][:, 1:2]))
                    ctx.ex(v, V, lambda: v.tensor_tensor(
                        out=rs2[p][:, 0:1], in0=rs2[p][:, 0:1],
                        in1=mv[p][:, 1:2], op=ALU.mult))
                    stats_tok[p] = ctx.ex(v, V, lambda: v.tensor_scalar(
                        out=rs2[p][:, 0:1], in0=rs2[p][:, 0:1], scalar1=1.0,
                        scalar2=EPS, op0=ALU.mult, op1=ALU.add))

                @block.scalar
                def _(a, t=t, p=p):
                    sqrt_tok[p] = ctx.ex(a, A, lambda: a.activation(
                        out=rs2[p][:, 2:4], in_=rs2[p][:, 0:2], func=AF.Sqrt),
                        deps=[stats_tok[p], fin_tok[p]])

            def emit_finish(t):
                p = t % 2
                row0 = s * SEQ + t * 128

                @block.vector
                def _(v, t=t, p=p):
                    ctx.ex(v, V, lambda: v.reciprocal(rs2[p][:, 0:1],
                                                      rs2[p][:, 2:3]),
                           deps=[sqrt_tok[p]])
                    ctx.ex(v, V, lambda: v.reciprocal(rs2[p][:, 1:2],
                                                      rs2[p][:, 3:4]))
                    ctx.ex(v, V, lambda: v.tensor_scalar(
                        out=xnb[:], in0=xbuf[p][:], scalar1=mv[p][:, 0:1],
                        scalar2=rs2[p][:, 1:2], op0=ALU.subtract,
                        op1=ALU.mult))
                    ctx.ex(v, V, lambda: v.tensor_reduce(
                        out=mxr[:], in_=xnb[:], axis=AX, op=ALU.max,
                        apply_absolute_value=True))
                    mx_tok[0] = ctx.ex(v, V, lambda: v.tensor_max(
                        mx[:], mx[:], mxr[:]))
                    ctx.ex(v, V, lambda: v.scalar_tensor_tensor(
                        out=hbf[p][:], in0=xnb[:], scalar=rs2[p][:, 0:1],
                        in1=wnb[:], op0=ALU.mult, op1=ALU.mult),
                        deps=[prep_tok if (s == 0 and t == 0) else None,
                              hxp[p],
                              cmm_z_done[0] if t < 2 else None])
                    x_rd[p] = ctx.ex(v, V, lambda: v.tensor_tensor(
                        out=sq[p][:], in0=xbuf[p][:], in1=xnb[:], op=ALU.add),
                        deps=[pstore[s][p],
                              fmm_done[0] if t < 2 else None])
                    fin_tok[p] = x_rd[p]

                @block.scalar
                def _(a, t=t, p=p):
                    outv = h_fm[:].rearrange("p (k q) -> p k q", q=SEQ)[
                        :, :, t * 128:(t + 1) * 128]
                    inv = hbf[p][:].rearrange("p (k c) -> p k c", c=128)
                    hxp[p] = ctx.dma(a, f"hxp{p}", outv, inv,
                                     deps=[ctx.sem(V).tok()], transpose=True)
                    hxp_all_toks[s][t] = hxp[p]

                @block.sync
                def _(sync, t=t, p=p, row0=row0):
                    pstore[s][p] = pstore_all[s][t] = ctx.dma(
                        sync, f"pst{p}", partial[row0:row0 + 128, :],
                        sq[p][:], deps=[x_rd[p]])

            emit_stats(0)
            for t in range(1, NT):
                emit_stats(t)
                emit_finish(t - 1)
            emit_finish(NT - 1)

            @block.gpsimd
            def _(g, s=s):
                for k in range(NKD):
                    ctx.dma(g, "wua", win[:, k * INNER:(k + 1) * INNER],
                            w_inT[k * 128:(k + 1) * 128, 0:INNER],
                            deps=[win_rd[0]] if k == 0 else ())
            wua_tok = ctx.sem("wua").tok()

            bmm_evac = [None, None]
            us_tok = [None]
            hxp_seq = [hxp_all_toks[s][NT - 1], hxp_all_toks[s][NT - 2]]
            for m in range(NJ):
                pb = m % 2

                @block.tensor
                def _(t_, s=s, m=m, pb=pb):
                    deps = ([wua_tok, ctx.sem("clmm").tok(),
                             ctx.sem("fpk").tok(), *hxp_seq]
                            if m == 0 else []) + [bmm_evac[pb]]
                    for d in deps:
                        if d is not None:
                            t_.wait_ge(d[0].h, d[1])
                    last = None
                    for tt in range(SEQ // 512):
                        base = pb * 2048 + tt * 512
                        for k in range(NKD):
                            last = t_.matmul(
                                ps_all[:, base:base + 512],
                                win[:, k * INNER + m * 128:k * INNER + (m + 1) * 128],
                                h_fm[:, k * SEQ + tt * 512:k * SEQ + (tt + 1) * 512],
                                start=(k == 0), stop=(k == NKD - 1))
                    ctx.sem("bmm").inc(last)

                @block.scalar
                def _(a, s=s, m=m, pb=pb):
                    ctx.ex(a, A, lambda: a.activation(
                        out=us[:, m * SEQ:(m + 1) * SEQ],
                        in_=ps_all[:, pb * 2048:(pb + 1) * 2048],
                        func=AF.Silu), deps=[ctx.sem("bmm").tok()])
                    bmm_evac[pb] = us_tok[0] = ctx.sem(A).tok()

            @block.gpsimd
            def _(g, s=s):
                g.wait_ge(ctx.sem("bmm").h, ctx.sem("bmm").n)
                for k in range(NKD):
                    ctx.dma(g, "wza", win[:, k * INNER:(k + 1) * INNER],
                            w_inT[k * 128:(k + 1) * 128, INNER:2 * INNER])
            wza_tok = ctx.sem("wza").tok()

            wdt_ld = [None, None]
            lam_evac = [None]
            z_evac = [None]
            s2st = [None, None]
            for j in range(NJ):
                pj = j % 2

                @block.gpsimd
                def _(g, s=s, j=j, pj=pj):
                    deps = [ctx.sem("clmm").tok() if j >= 2 else None]
                    wdt_ld[pj] = ctx.dma(
                        g, f"wdt{pj}",
                        wdtb[pj][:].rearrange("p (k j) -> p k j", j=128),
                        w_dtT[:, j * 128:(j + 1) * 128].rearrange(
                            "(k p) j -> p k j", p=128),
                        deps=deps)

                @block.tensor
                def _(t_, s=s, j=j, pj=pj):
                    deps = [wdt_ld[pj], lam_evac[0],
                            us_tok[0] if j == 0 else None]
                    for d in deps:
                        if d is not None:
                            t_.wait_ge(d[0].h, d[1])
                    last = None
                    for tt in range(SEQ // 512):
                        for k in range(NJ):
                            last = t_.matmul(
                                ps_all[:, tt * 512:(tt + 1) * 512],
                                wdtb[pj][:, k * 128:(k + 1) * 128],
                                us[:, k * SEQ + tt * 512:k * SEQ + (tt + 1) * 512],
                                start=(k == 0), stop=(k == NJ - 1))
                    ctx.sem("clmm").inc(last)
                    deps = [wza_tok if j == 0 else None, z_done[0]]
                    for d in deps:
                        if d is not None:
                            t_.wait_ge(d[0].h, d[1])
                    last = None
                    for tt in range(SEQ // 512):
                        for k in range(NKD):
                            last = t_.matmul(
                                ps_all[:, 2048 + tt * 512:2048 + (tt + 1) * 512],
                                win[:, k * INNER + j * 128:k * INNER + (j + 1) * 128],
                                h_fm[:, k * SEQ + tt * 512:k * SEQ + (tt + 1) * 512],
                                start=(k == 0), stop=(k == NKD - 1))
                    cmm_z_done[0] = ctx.sem("czmm").inc(last)

                @block.scalar
                def _(a, s=s, j=j, pj=pj):
                    ctx.ex(a, A, lambda: a.activation(
                        out=lam[:], in_=ps_all[:, 0:2048], func=AF.Sigmoid,
                        bias=bdt[:, j:j + 1]),
                        deps=[ctx.sem("clmm").tok(), scan_tok[0],
                              fmm_done[0] if j == 0 else None,
                              pa_rd[0] if j == 0 else None,
                              pa_rd[1] if j == 0 else None])
                    lam_evac[0] = ctx.sem(A).tok()
                    ctx.ex(a, A, lambda: a.activation(
                        out=siluz[:], in_=ps_all[:, 2048:4096], func=AF.Sigmoid),
                        deps=[ctx.sem("czmm").tok(), s2_tok[0],
                              ost[0] if j == 0 else None,
                              ost[1] if j == 0 else None])
                    z_evac[0] = ctx.sem(A).tok()

                @block.vector
                def _(v, s=s, j=j, pj=pj):
                    ctx.ex(v, V, lambda: v.scalar_tensor_tensor(
                        out=bneg[:], in0=lam[:], scalar=1.0,
                        in1=us[:, j * SEQ:(j + 1) * SEQ],
                        op0=ALU.subtract, op1=ALU.mult),
                        deps=[lam_evac[0],
                              pstore[s][0] if j == 0 else None,
                              pstore[s][1] if j == 0 else None,
                              x_ld[0] if j == 0 else None,
                              x_ld[1] if j == 0 else None])
                    scan_tok[0] = ctx.ex(v, V, lambda: v.tensor_tensor_scan(
                        out=sneg[:], data0=lam[:], data1=bneg[:], initial=0.0,
                        op0=ALU.mult, op1=ALU.add))
                    ctx.ex(v, V, lambda: v.tensor_tensor(
                        out=siluz[:], in0=siluz[:], in1=ps_all[:, 2048:4096],
                        op=ALU.mult), deps=[z_evac[0]])
                    z_done[0] = ctx.sem(V).tok()
                    s2_tok[0] = ctx.ex(v, V, lambda: v.scalar_tensor_tensor(
                        out=s2b[pj][:], in0=sneg[:], scalar=-1.0, in1=siluz[:],
                        op0=ALU.mult, op1=ALU.mult),
                        deps=[s2st[pj]])

                @block.sync
                def _(sync, s=s, j=j, pj=pj):
                    s2st[pj] = ctx.dma(sync, f"s2st{pj}",
                                       s2_dram[s, j * 128:(j + 1) * 128, :],
                                       s2b[pj][:], deps=[s2_tok[0]])

            @block.gpsimd
            def _(g, s=s):
                g.wait_ge(ctx.sem("clmm").h, ctx.sem("clmm").n)
                for k in range(NJ):
                    ctx.dma(g, "woa", wob[:, k * D:(k + 1) * D],
                            w_outT[k * 128:(k + 1) * 128, :])
            woa_tok = ctx.sem("woa").tok()

            for g_ in range(4):
                for k in range(NJ):
                    pk = k % 2

                    @block.sync
                    def _(sync, s=s, g_=g_, k=k, pk=pk):
                        deps = [s2_rd[pk],
                                s2st[0] if (g_ == 0 and k == 0) else None,
                                s2st[1] if (g_ == 0 and k == 0) else None]
                        s2_ld[pk] = ctx.dma(
                            sync, f"s2ld{pk}", s2k[pk][:],
                            s2_dram[s, k * 128:(k + 1) * 128,
                                    g_ * 512:(g_ + 1) * 512], deps=deps)

                    @block.tensor
                    def _(t_, s=s, g_=g_, k=k, pk=pk):
                        deps = [s2_ld[pk]] +                                ([woa_tok] if (g_ == 0 and k == 0) else []) +                                (ev_done if k == 0 else [])
                        for d in deps:
                            if d is not None:
                                t_.wait_ge(d[0].h, d[1])
                        last = None
                        for c in range(4):
                            for nn in range(2):
                                last = t_.matmul(
                                    ps_all[:, c * 1024 + nn * 512:
                                           c * 1024 + (nn + 1) * 512],
                                    s2k[pk][:, c * 128:(c + 1) * 128],
                                    wob[:, k * D + nn * 512:k * D + (nn + 1) * 512],
                                    start=(k == 0), stop=(k == NJ - 1))
                        s2_rd[pk] = ctx.sem("fpk").inc(last)
                        if k == NJ - 1:
                            fmm_done[0] = ctx.sem("fpk").tok()

                for c in range(4):
                    pc = c % 2
                    row0 = s * SEQ + g_ * 512 + c * 128
                    tile_idx = g_ * 4 + c

                    @block.scalar
                    def _(a, s=s, c=c, pc=pc, row0=row0, tile_idx=tile_idx):
                        pa_ld[pc] = ctx.dma(a, f"pld{pc}", padd[pc][:],
                                            partial[row0:row0 + 128, :],
                                            deps=[pa_rd[pc],
                                                  pstore_all[s][tile_idx],
                                                  lam_evac[0] if tile_idx < 2 else None,
                                                  z_evac[0] if tile_idx < 2 else None,
                                                  s2_tok[0] if tile_idx < 2 else None])

                    @block.vector
                    def _(v, s=s, c=c, pc=pc):
                        pa_rd[pc] = ev_done[c] = ctx.ex(
                            v, V, lambda: v.tensor_tensor(
                                out=outb[pc][:], in0=padd[pc][:],
                                in1=ps_all[:, c * 1024:(c + 1) * 1024],
                                op=ALU.add),
                            deps=[pa_ld[pc], ctx.sem("fpk").tok(), ost[pc]])

                    @block.sync
                    def _(sync, s=s, pc=pc, row0=row0):
                        ost[pc] = ctx.dma(sync, f"ost{pc}",
                                          partial[row0:row0 + 128, :],
                                          outb[pc][:], deps=[ev_done[c]])
            win_rd[0] = ctx.sem("czmm").tok()

        ctx.colmax_scratch = xnb
        emit_colmax_to_out(ctx, block, mx, ident, m1loc, 1, mx_tok[0], "m1",
                           ident_tok=ld0, ps_view=ps_all,
                           extra_dep=ctx.sem(V).tok())

        @block.sync
        def _(sync):
            for sname in ("ost0", "ost1", "st_m1", "pst0", "pst1"):
                sync.wait_ge(ctx.sem(sname).h, ctx.sem(sname).n)

    return ctx.nc


_CACHE = {}


def _get(name, builder):
    if name not in _CACHE:
        _CACHE[name] = builder()
    return _CACHE[name]


def _make_e(nblk, nchunk):
    e = np.zeros((nblk, nchunk * 128), np.float32)
    for c in range(nchunk):
        for p in range(128):
            e[4 * c + p // 32, c * 128 + p] = 1.0
    return e


def _run(name, builder, in_maps):
    nc = _get(name, builder)
    res = run_bass_kernel_spmd(nc, in_maps, core_ids=list(range(NCORES)))
    return res.results


def kernel(L0_attn_x, L0_attn_w_norm, L0_attn_W_in, L0_attn_W_dt, L0_attn_b_dt,
           L0_attn_W_out, L0_ffn_w_gate, L0_ffn_w_up, L0_ffn_w_down):
    f32c = lambda a: np.ascontiguousarray(np.asarray(a), dtype=np.float32)
    x = f32c(L0_attn_x)
    xs = [np.ascontiguousarray(x[2 * i:2 * i + 2].reshape(NTOK, D))
          for i in range(NCORES)]
    w_inT = np.ascontiguousarray(f32c(L0_attn_W_in).T)
    w_dtT = np.ascontiguousarray(f32c(L0_attn_W_dt).T)
    w_outT = np.ascontiguousarray(f32c(L0_attn_W_out).T)
    w_norm = f32c(L0_attn_w_norm).reshape(1, D)
    b_dt = f32c(L0_attn_b_dt).reshape(1, INNER)
    wup_nat = f32c(L0_ffn_w_up)
    wgate_nat = f32c(L0_ffn_w_gate)
    wdown_nat = f32c(L0_ffn_w_down)
    wupT = np.ascontiguousarray(wup_nat.T)
    wgateT = np.ascontiguousarray(wgate_nat.T)
    wdownT = np.ascontiguousarray(wdown_nat.T)
    ident = np.eye(128, dtype=np.float32)
    e32 = _make_e(32, 8)
    e96 = _make_e(96, 24)

    r1 = _run("L1", build_L1, [dict(x=xs[i], w_inT=w_inT, w_dtT=w_dtT,
                                    w_outT=w_outT, w_norm=w_norm, b_dt=b_dt,
                                    c_ident=ident) for i in range(NCORES)])
    m1all = np.concatenate([r1[i]["m1loc"].reshape(1, 1)
                            for i in range(NCORES)], axis=1)

    r2 = _run("L2", build_L2, [dict(x=xs[i], m1all=m1all, wup_nat=wup_nat,
                                    wgate_nat=wgate_nat, c_ident=ident)
                               for i in range(NCORES)])
    m23 = np.stack([r2[i]["m23loc"].reshape(2) for i in range(NCORES)], axis=1)
    m2all = np.ascontiguousarray(m23[0:1, :])
    m3all = np.ascontiguousarray(m23[1:2, :])

    r3 = _run("L3", build_L3, [dict(gate=r2[i]["gate"], m3all=m3all,
                                    c_ident=ident) for i in range(NCORES)])
    m4all = np.concatenate([r3[i]["m4loc"].reshape(1, 1)
                            for i in range(NCORES)], axis=1)

    r4 = _run("L4", build_L4, [dict(gate2=r3[i]["gate2"], up=r2[i]["up"],
                                    m2all=m2all, m4all=m4all, c_ident=ident)
                               for i in range(NCORES)])
    m5all = np.concatenate([r4[i]["m5loc"].reshape(1, 1)
                            for i in range(NCORES)], axis=1)

    r5 = _run("L5", build_L5, [dict(o_in=r4[i]["o_out"], m5all=m5all,
                                    wdown_nat=wdown_nat,
                                    partial=r1[i]["partial"], c_ident=ident)
                               for i in range(NCORES)])

    out = np.empty((16, SEQ, D), np.float32)
    for i in range(NCORES):
        out[2 * i:2 * i + 2] = r5[i]["outp"].reshape(2, SEQ, D)
    return out
